# version 19
# baseline (speedup 1.0000x reference)
from contextlib import ExitStack

import numpy as np
import ml_dtypes

import concourse.bass as bass
import concourse.mybir as mybir
from concourse.bass_utils import run_bass_kernel_spmd

F32 = mybir.dt.float32
BF16 = mybir.dt.bfloat16
AF = mybir.ActivationFunctionType
ALU = mybir.AluOpType

P = 128
D = 2048
R = 1024
DT = R // P
NT = D // P
MQ = 512
NQ = D // MQ
J = 256
JT = J // P
II = 5632
IT = II // P
HT = D // P
EPS = 1e-5
BLKS = [8, 8, 8, 8, 8, 4]

ENGS = ("sync", "gpsimd", "scalar", "vector", "tensor")
SEMS = ("sXI", "sXT", "sSM", "sSP", "sRL", "sRL2", "sOUT", "sXTB", "sXTT",
        "sWB0", "sAMX", "sDS", "sGW0", "sGW1", "sUW0", "sUW1", "sDW", "sPE",
        "sDV", "sAC", "sGP")


class Plan:

    def __init__(self):
        self.cnt = {s: 0 for s in SEMS}
        self.ops = None
        self.last_wait = {e: {} for e in ENGS}

    def new_block(self):
        self.ops = {e: [] for e in ENGS}

    def em(self, eng, fn=None, w=(), inc=None, n=1):
        self.ops[eng].append((tuple(w), fn, inc, n))
        if inc is not None:
            self.cnt[inc] += n
            return self.cnt[inc]
        return None

    def gop(self, fn, w=()):
        return self.em("gpsimd", fn, w=list(w) + [("sGP", self.cnt["sGP"])],
                       inc="sGP")

    def vop(self, fn, w=()):
        return self.em("vector", fn, w=list(w) + [("sDV", self.cnt["sDV"])],
                       inc="sDV")

    def aop(self, fn, w=()):
        return self.em("scalar", fn, w=list(w) + [("sAC", self.cnt["sAC"])],
                       inc="sAC")

    def dma(self, sem, q, out, in_, w=(), accum=False):
        if accum:
            return self.em(q, lambda e, o=out, i=in_: e.dma_start(
                o, i, accum_op=ALU.add), w=w, inc=sem, n=16)
        return self.em(q, lambda e, o=out, i=in_: e.dma_start(o, i), w=w,
                       inc=sem, n=16)

    def v(self, s):
        return self.cnt[s]

    def emit(self, nc, sems):
        ops = self.ops
        with nc.Block() as blk:
            for ename in ENGS:
                if not ops[ename]:
                    continue

                def mk(lst=ops[ename], ename=ename):
                    def f(eng):
                        lw = self.last_wait[ename]
                        for w, fn, inc, n in lst:
                            for s, val in w:
                                if val > lw.get(s, 0):
                                    eng.wait_ge(sems[s], val)
                                    lw[s] = val
                            if fn is not None:
                                ins = fn(eng)
                                if inc is not None:
                                    ins.then_inc(sems[inc], n)
                    return f

                getattr(blk, ename)(mk())


def build_nc():
    nc = bass.Bass("TRN2", target_bir_lowering=False, debug=False)

    t_x = nc.dram_tensor("x", [R, D], F32, kind="ExternalInput")
    t_xT = nc.dram_tensor("xT", [D, R], F32, kind="ExternalInput")
    t_xtop = nc.dram_tensor("xtop", [J, D], F32, kind="ExternalInput")
    t_xtopT = nc.dram_tensor("xtopT", [D, J], F32, kind="ExternalInput")
    t_blT = nc.dram_tensor("blT", [J, J], F32, kind="ExternalInput")
    t_clT = nc.dram_tensor("clT", [J, J], F32, kind="ExternalInput")
    t_amixT = nc.dram_tensor("amixT", [J, R], F32, kind="ExternalInput")
    t_rl = nc.dram_tensor("rl", [P, 1], F32, kind="ExternalInput")
    t_mask = nc.dram_tensor("maskc", [P, DT], F32, kind="ExternalInput")
    t_lastm = nc.dram_tensor("lastm", [P, 1], F32, kind="ExternalInput")
    t_ident = nc.dram_tensor("ident", [P, P], BF16, kind="ExternalInput")
    t_gw = nc.dram_tensor("gw", [IT, P, HT, P], F32, kind="ExternalInput")
    t_uw = nc.dram_tensor("uw", [IT, P, HT, P], F32, kind="ExternalInput")
    t_dw = nc.dram_tensor("dw", [II, D], F32, kind="ExternalInput")
    t_gs = nc.dram_tensor("gsc", [P, IT], F32, kind="ExternalInput")
    t_us = nc.dram_tensor("usc", [P, IT], F32, kind="ExternalInput")
    t_ds = nc.dram_tensor("dsr", [P, D], F32, kind="ExternalInput")
    t_y = nc.dram_tensor("y", [R, D], F32, kind="ExternalOutput")

    es = ExitStack()
    sems = {s: es.enter_context(nc.semaphore(s)) for s in SEMS}
    pl = Plan()

    glob = ExitStack()

    def g_sb(name, shape, dtype=F32):
        return glob.enter_context(nc.sbuf_tensor(name, shape, dtype))

    muvar1 = g_sb("muvar1", [P, DT, 2])
    r1 = g_sb("r1", [P, DT])
    A2 = g_sb("A2", [P, DT])
    A3 = g_sb("A3", [P, DT])
    A4 = g_sb("A4", [P, DT])
    srow = g_sb("srow", [P, DT])
    rl_sb = g_sb("rl_sb", [P, 1])
    mask_sb = g_sb("mask_sb", [P, DT])
    lm_sb = g_sb("lm_sb", [P, 1])
    Xmv = g_sb("Xmv", [P, JT, 2])
    Xr = g_sb("Xr", [P, JT])
    nrmuX = g_sb("nrmuX", [P, JT])
    bn6 = g_sb("bn6", [P, 4, 6])
    lnt = g_sb("lnt", [P, DT])
    sc2 = g_sb("sc2", [P, JT])
    muvF = g_sb("muvF", [P, DT, 2])
    rF = g_sb("rF", [P, DT])
    muvN = g_sb("muvN", [P, 2])
    rN = g_sb("rN", [P, 1])
    ones_c = g_sb("ones_c", [P, 1], BF16)
    ones_r = g_sb("ones_r", [1, P], BF16)
    gs_sb = g_sb("gs_sb", [P, IT])
    us_sb = g_sb("us_sb", [P, IT])
    ident_sb = g_sb("ident_sb", [P, P], BF16)
    muvW = g_sb("muvW", [P, 2])
    rW = g_sb("rW", [P, 1])
    nacc = g_sb("nacc", [P, DT, NQ])
    sqacc = g_sb("sqacc", [P, DT, NQ])
    s1acc = g_sb("s1acc", [P, DT])
    s2acc = g_sb("s2acc", [P, DT])
    muN8 = g_sb("muN8", [P, DT])
    rN8 = g_sb("rN8", [P, DT])
    ngN8 = g_sb("ngN8", [P, DT])
    muF8 = g_sb("muF8", [P, DT])
    ngF8 = g_sb("ngF8", [P, DT])
    t1c = g_sb("t1c", [P, 1])
    t3c = g_sb("t3c", [P, 1])
    tstat = g_sb("tstat", [P, DT])
    t2c = g_sb("t2c", [P, 1])
    wsum = g_sb("wsum", [P, DT])
    wsq = g_sb("wsq", [P, DT])
    ngW = g_sb("ngW", [P, 1])
    nA2 = g_sb("nA2", [P, DT])
    muW8 = g_sb("muW8", [P, DT])
    rW8 = g_sb("rW8", [P, DT])
    ngW8 = g_sb("ngW8", [P, DT])
    cst = g_sb("cst", [P, 3])
    c_eps = cst[:, 0:1]
    c_nh = cst[:, 1:2]

    sx = ExitStack()
    xbuf = sx.enter_context(nc.sbuf_tensor("xbuf", [P, DT, D], F32))
    s01 = ExitStack()
    xTb = s01.enter_context(nc.sbuf_tensor("xTb", [P, NT, R], BF16, side="right"))
    ub = s01.enter_context(nc.sbuf_tensor("ub", [P, JT, D], BF16, side="right"))
    wb = s01.enter_context(nc.sbuf_tensor("wb", [P, JT, D], BF16, side="right"))
    xtopTb = s01.enter_context(
        nc.sbuf_tensor("xtopTb", [P, NT, J], BF16, side="right"))
    amixb = s01.enter_context(
        nc.sbuf_tensor("amixb", [P, JT, R], BF16, side="right"))
    sT = ExitStack()
    attn_out = sT.enter_context(nc.sbuf_tensor("attn_out", [P, DT, D], BF16))
    s0 = ExitStack()
    xtop_sb = s0.enter_context(nc.sbuf_tensor("xtop_sb", [P, JT, D], F32))
    Xn = s0.enter_context(nc.sbuf_tensor("Xn", [P, JT, D], BF16))
    bltb = s0.enter_context(nc.sbuf_tensor("bltb", [P, JT, J], BF16))
    cltb = s0.enter_context(nc.sbuf_tensor("cltb", [P, JT, J], BF16))
    ps01 = ExitStack()
    ps_a = ps01.enter_context(nc.psum_tensor("ps_a", [P, 2, MQ], F32))
    ps_t = ps01.enter_context(nc.psum_tensor("ps_t", [P, 3, MQ], F32))
    ps_sh = ps01.enter_context(nc.psum_tensor("ps_sh", [P, 2, MQ], F32))

    def ln_stats(src2048, muvar_out, w_extra=()):
        for c in range(4):
            pl.vop(lambda e, c=c, s=src2048: e.bn_stats(
                bn6[:, c, :], s[:, c * 512:(c + 1) * 512]), w=w_extra)
        pl.vop(lambda e, m=muvar_out: e.bn_aggr(
            m, bn6[:, :, :].rearrange("p a b -> p (a b)")))

    pl.new_block()

    pl.dma("sXI", "sync", ident_sb[:, :], t_ident.ap())

    pl.dma("sXT", "scalar", xtop_sb[:, 0, :], t_xtop.ap()[0:P, :])
    pl.dma("sXT", "sync", xtop_sb[:, 1, :], t_xtop.ap()[P:2 * P, :])
    XT = pl.v("sXT")
    for dt in range(DT):
        pl.dma("sXI", "sync", xbuf[:, dt, :],
               t_x.ap()[dt * P:(dt + 1) * P, :])
    XI = pl.v("sXI")
    pl.dma("sSM", "scalar", rl_sb[:, :], t_rl.ap())
    pl.dma("sSM", "scalar", mask_sb[:, :], t_mask.ap())
    pl.dma("sSM", "scalar", lm_sb[:, :], t_lastm.ap())
    pl.dma("sSM", "scalar", gs_sb[:, :], t_gs.ap())
    pl.dma("sSM", "scalar", us_sb[:, :], t_us.ap())
    SM = pl.v("sSM")

    pl.dma("sWB0", "gpsimd", bltb[:, :, :],
           t_blT.ap().rearrange("(jt p) j -> p jt j", p=P))
    pl.dma("sWB0", "gpsimd", cltb[:, :, :],
           t_clT.ap().rearrange("(jt p) j -> p jt j", p=P))
    WB0 = pl.v("sWB0")
    for k in range(2):
        pl.dma("sXTT", "gpsimd", xtopTb[:, 8 * k:8 * k + 8, :],
               t_xtopT.ap()[8 * k * P:8 * (k + 1) * P, :].rearrange(
                   "(nt p) j -> p nt j", p=P))
    XTT = pl.v("sXTT")
    pl.dma("sAMX", "gpsimd", amixb[:, :, :],
           t_amixT.ap().rearrange("(jt p) d -> p jt d", p=P))
    AMX = pl.v("sAMX")
    pl.em("gpsimd", fn=None,
          w=[("sWB0", WB0), ("sXTT", XTT), ("sAMX", AMX)])
    for k in range(4):
        pl.dma("sXTB", "gpsimd", xTb[:, 4 * k:4 * k + 4, :],
               t_xT.ap()[4 * k * P:4 * (k + 1) * P, :].rearrange(
                   "(nt p) d -> p nt d", p=P))
    XTB = pl.v("sXTB")

    pl.vop(lambda e: e.memset(cst[:, 0:1], EPS))
    pl.vop(lambda e: e.memset(cst[:, 1:2], -0.5))
    pl.vop(lambda e: e.memset(cst[:, 2:3], -1.0))
    cstv = pl.v("sDV")
    pl.vop(lambda e: e.memset(ones_c[:, :], 1.0))
    pl.vop(lambda e: e.memset(ones_r[:, :], 1.0))

    pl.aop(lambda e: e.activation(sc2[:, 0:1], cst[:, 0:1], AF.Ln,
                                  bias=c_eps), w=[("sDV", cstv)])
    pl.aop(lambda e: e.activation(sc2[:, 0:1], sc2[:, 0:1], AF.Exp,
                                  scale=c_nh))

    for jt in range(JT):
        ln_stats(xtop_sb[:, jt, :], Xmv[:, jt, :], w_extra=[("sXT", XT)])
    xstat = pl.v("sDV")
    pl.aop(lambda e: e.activation(sc2[:, :], Xmv[:, :, 1], AF.Ln, bias=c_eps),
           w=[("sDV", xstat)])
    pl.aop(lambda e: e.activation(Xr[:, :], sc2[:, :], AF.Exp, scale=c_nh))
    xr_done = pl.v("sAC")
    for jt in range(JT):
        pl.vop(lambda e, jt=jt: e.tensor_scalar(
            Xn[:, jt, :], xtop_sb[:, jt, :],
            Xmv[:, jt, 0:1], Xr[:, jt:jt + 1], ALU.subtract, ALU.mult),
            w=[("sAC", xr_done)])
    xn_done = pl.v("sDV")
    pl.vop(lambda e: e.tensor_tensor(nrmuX[:, :], Xmv[:, :, 0], Xr[:, :],
                                     ALU.mult))
    pl.vop(lambda e: e.tensor_scalar_mul(nrmuX[:, :], nrmuX[:, :], -1.0))

    uw_bank_free = {}
    for (wsb, dst) in ((bltb, ub), (cltb, wb)):
        for it2 in range(JT):
            for c4 in range(4):
                bk = (it2 * 4 + c4) % 2
                w = [("sWB0", WB0), ("sDV", xn_done)]
                if bk in uw_bank_free:
                    w.append(("sAC", uw_bank_free[bk]))
                for jt in range(JT):
                    pl.em("tensor",
                          lambda e, wsb=wsb, it2=it2, c4=c4, jt=jt, bk=bk:
                          e.matmul(ps_a[:, bk, :],
                                   wsb[:, jt, it2 * P:(it2 + 1) * P],
                                   Xn[:, jt, c4 * 512:(c4 + 1) * 512],
                                   start=(jt == 0), stop=(jt == 1)),
                          w=(w if jt == 0 else ()),
                          inc=("sPE" if jt == 1 else None))
                mm = pl.v("sPE")
                pl.aop(lambda e, dst=dst, it2=it2, c4=c4, bk=bk:
                       e.activation(dst[:, it2, c4 * 512:(c4 + 1) * 512],
                                    ps_a[:, bk, :], AF.Copy),
                       w=[("sPE", mm)])
                uw_bank_free[bk] = pl.v("sAC")
    uw_done_pe = pl.v("sPE")
    uw_done_ac = pl.v("sAC")


    pl.emit(nc, sems)
    s0.close()

    s1 = ExitStack()
    att2 = s1.enter_context(nc.sbuf_tensor("att2", [P, 2, NT, MQ], BF16))
    colsumB = s1.enter_context(nc.sbuf_tensor("colsumB", [P, MQ], F32))
    cs_row = s1.enter_context(nc.sbuf_tensor("cs_row", [1, MQ], BF16))
    txb = s1.enter_context(nc.sbuf_tensor("txb", [P, JT, MQ], BF16))
    txf = s1.enter_context(nc.sbuf_tensor("txf", [P, MQ], F32))
    tmpe = s1.enter_context(nc.sbuf_tensor("tmpe", [P, 2, MQ], F32))
    tmpe2 = s1.enter_context(nc.sbuf_tensor("tmpe2", [P, MQ], F32))
    tmpn = s1.enter_context(nc.sbuf_tensor("tmpn", [P, D], BF16))

    pl.new_block()

    for dt in range(DT):
        ln_stats(xbuf[:, dt, :], muvar1[:, dt, :], w_extra=[("sXI", XI)])
    ln1_done = pl.v("sDV")
    coeffs_done = [0]

    def emit_r1_coeffs():
        pl.aop(lambda e: e.activation(lnt[:, :], muvar1[:, :, 1], AF.Ln,
                                      bias=c_eps), w=[("sDV", ln1_done)])
        pl.aop(lambda e: e.activation(r1[:, :], lnt[:, :], AF.Exp,
                                      scale=c_nh))
        r1_done = pl.v("sAC")
        pl.vop(lambda e: e.tensor_tensor(A2[:, :], muvar1[:, :, 0], r1[:, :],
                                         ALU.mult), w=[("sAC", r1_done)])
        pl.vop(lambda e: e.tensor_scalar(srow[:, :], mask_sb[:, :],
                                         rl_sb[:, 0:1], None, ALU.mult),
               w=[("sSM", SM)])
        pl.vop(lambda e: e.tensor_tensor(A3[:, :], srow[:, :], r1[:, :],
                                         ALU.mult))
        pl.vop(lambda e: e.tensor_tensor(A4[:, :], A2[:, :], srow[:, :],
                                         ALU.mult))
        pl.vop(lambda e: e.tensor_scalar_mul(A4[:, :], A4[:, :], -1.0))
        pl.vop(lambda e: e.tensor_scalar_mul(nA2[:, :], A2[:, :], -1.0))
        coeffs_done[0] = pl.v("sDV")

    relu_cnt = {}
    t0_last = {}
    sh_free = {}
    pst_free = {}
    epi_done = {}
    epi_v2 = {0: 0, 1: 0}
    epi_i = [0]
    sq_done = {}
    sp_done = []
    x1_done = {}
    x1sq_done = {}

    def emit_x1_chain(dt):
        qp = NQ - 1
        qsp = slice(qp * MQ, (qp + 1) * MQ)
        pl.aop(lambda e, dt=dt, qp=qp, qsp=qsp: e.activation(
            tmpn[:, 0:MQ], attn_out[:, dt, qsp], AF.Square,
            accum_out=sqacc[:, dt, qp:qp + 1]),
            w=[("sDV", epi_done[(qp, dt)])])
        sq_done[(qp, dt)] = pl.v("sAC")

    for q in range(NQ):
        qb = q % 2
        qs = slice(q * MQ, (q + 1) * MQ)
        for nt in range(NT):
            bk = nt % 2
            w = []
            if (q, nt - 2) in relu_cnt:
                w.append(relu_cnt[(q, nt - 2)])
            elif q > 0:
                w.append(relu_cnt[(q - 1, NT - 2 + nt)])
            else:
                w.append(("sAC", uw_done_ac))
            for dd in range(JT):
                pl.em("tensor",
                      lambda e, nt=nt, dd=dd, bk=bk, qs=qs:
                      e.matmul(ps_a[:, bk, :], ub[:, dd, nt * P:(nt + 1) * P],
                               wb[:, dd, qs], start=(dd == 0), stop=(dd == 1)),
                      w=(w if dd == 0 else ()),
                      inc=("sPE" if dd == 1 else None))
            mm = pl.v("sPE")
            wr = [("sPE", mm)]
            if q >= 2:
                wr.append(("sPE", t0_last[q - 2]))
            if nt % 2 == 0 or q == 0:
                pl.aop(lambda e, nt=nt, qb=qb, bk=bk:
                       e.activation(att2[:, qb, nt, :], ps_a[:, bk, :],
                                    AF.Relu), w=wr)
                relu_cnt[(q, nt)] = ("sAC", pl.v("sAC"))
            else:
                pl.vop(lambda e, nt=nt, qb=qb, bk=bk:
                       e.tensor_scalar_max(att2[:, qb, nt, :],
                                           ps_a[:, bk, :], 0.0), w=wr)
                relu_cnt[(q, nt)] = ("sDV", pl.v("sDV"))
        att_fix = 0
        if q == NQ - 1:
            pl.vop(lambda e, qb=qb: e.tensor_scalar(
                att2[:, qb, NT - 1, MQ - 1:MQ],
                att2[:, qb, NT - 1, MQ - 1:MQ],
                lm_sb[:, 0:1], None, ALU.mult),
                w=[relu_cnt[(q, NT - 1)], relu_cnt[(q, NT - 2)], ("sSM", SM)])
            att_fix = pl.v("sDV")

        w0 = [relu_cnt[(q, NT - 1)], relu_cnt[(q, NT - 2)]]
        if att_fix:
            w0.append(("sDV", att_fix))
        if 0 in sh_free:
            w0.append(sh_free[0])
        for nt in range(NT):
            pl.em("tensor",
                  lambda e, nt=nt, qb=qb:
                  e.matmul(ps_sh[0:1, 0, :], ones_c[:, :], att2[:, qb, nt, :],
                           start=(nt == 0), stop=(nt == NT - 1)),
                  w=(w0 if nt == 0 else ()),
                  inc=("sPE" if nt == NT - 1 else None))
        cs_mm = pl.v("sPE")
        pl.aop(lambda e: e.activation(cs_row[:, :], ps_sh[0:1, 0, :], AF.Copy),
               w=[("sPE", cs_mm)])
        cs_ev = pl.v("sAC")
        if q == 0:
            emit_r1_coeffs()

        t0_mm_val = {}

        def t0_chain(dt):
            bidx = dt % 3
            wt0 = [("sXTB", XTB)]
            if bidx in pst_free:
                wt0.append(("sDV", pst_free[bidx]))
            for nt in range(NT):
                pl.em("tensor",
                      lambda e, dt=dt, nt=nt, qb=qb, bidx=bidx:
                      e.matmul(ps_t[:, bidx, :],
                               xTb[:, nt, dt * P:(dt + 1) * P],
                               att2[:, qb, nt, :],
                               start=(nt == 0), stop=(nt == NT - 1)),
                      w=(wt0 if nt == 0 else ()),
                      inc=("sPE" if nt == NT - 1 else None))
            t0_mm_val[dt] = pl.v("sPE")

        def epilogue(dt):
            bidx = dt % 3
            sl = epi_i[0] % 2
            epi_i[0] += 1
            pl.aop(lambda e, dt=dt, qs=qs, sl=sl: e.activation(
                tmpe[:, sl, :], xbuf[:, dt, qs], AF.Identity,
                scale=r1[:, dt:dt + 1], bias=nA2[:, dt:dt + 1]),
                w=[("sDV", max(epi_v2[sl], coeffs_done[0]))])
            a1 = pl.v("sAC")
            pl.vop(lambda e, dt=dt, sl=sl: e.scalar_tensor_tensor(
                tmpe2[:, :], colsumB[:, :], A4[:, dt:dt + 1], tmpe[:, sl, :],
                ALU.mult, ALU.add), w=[("sAC", a1)])
            epi_v2[sl] = pl.v("sDV")
            if dt < 2:
                pl.vop(lambda e, dt=dt: e.tensor_tensor(
                    tmpe2[:, :], tmpe2[:, :], ps_sh[:, dt, :], ALU.add),
                    w=[("sPE", aadd_mm[dt])])
                sh_free[dt] = ("sDV", pl.v("sDV"))
            pl.vop(lambda e, dt=dt, qs=qs, bidx=bidx, q=q:
                e.scalar_tensor_tensor(
                    attn_out[:, dt, qs], ps_t[:, bidx, :], A3[:, dt:dt + 1],
                    tmpe2[:, :], ALU.mult, ALU.add,
                    accum_out=nacc[:, dt, q:q + 1]),
                w=[("sPE", t0_mm_val[dt])])
            pst_free[bidx] = pl.v("sDV")
            epi_done[(q, dt)] = pl.v("sDV")
            if q >= 1:
                qp = q - 1
                qsp = slice(qp * MQ, (qp + 1) * MQ)
                pl.aop(lambda e, dt=dt, qp=qp, qsp=qsp: e.activation(
                    tmpn[:, 0:MQ], attn_out[:, dt, qsp], AF.Square,
                    accum_out=sqacc[:, dt, qp:qp + 1]),
                    w=[("sDV", epi_done[(qp, dt)])])
                sq_done[(qp, dt)] = pl.v("sAC")
            if q == NQ - 1:
                emit_x1_chain(dt)

        def tx_chain(jt):
            wtx = ([("sAC", cs_ev)] if jt == 0 else [("sDV", csb_done)])
            wtx.append(("sXTT", XTT))
            for nt in range(NT):
                pl.em("tensor",
                      lambda e, jt=jt, nt=nt, qb=qb:
                      e.matmul(ps_sh[:, jt, :],
                               xtopTb[:, nt, jt * P:(jt + 1) * P],
                               att2[:, qb, nt, :],
                               start=(nt == 0), stop=(nt == NT - 1)),
                      w=(wtx if nt == 0 else ()),
                      inc=("sPE" if nt == NT - 1 else None))
            tx_mm = pl.v("sPE")
            pl.vop(lambda e, jt=jt: e.tensor_scalar(
                txf[:, :], ps_sh[:, jt, :], Xr[:, jt:jt + 1], None, ALU.mult),
                w=[("sPE", tx_mm)])
            pl.vop(lambda e, jt=jt: e.scalar_tensor_tensor(
                txb[:, jt, :], colsumB[:, :], nrmuX[:, jt:jt + 1], txf[:, :],
                ALU.mult, ALU.add))

        t0_chain(2)
        wb1 = [("sAC", cs_ev)]
        if 1 in sh_free:
            wb1.append(sh_free[1])
        pl.em("tensor",
              lambda e: e.matmul(ps_sh[:, 1, :], ones_r[:, :], cs_row[:, :],
                                 start=True, stop=True),
              w=wb1, inc="sPE")
        bc_mm = pl.v("sPE")
        t0_chain(3)
        pl.vop(lambda e: e.tensor_copy(colsumB[:, :], ps_sh[:, 1, :]),
               w=[("sPE", bc_mm)])
        csb_done = pl.v("sDV")
        epilogue(2)
        tx_chain(0)
        t0_chain(4)
        epilogue(3)
        tx_chain(1)
        txb_done = pl.v("sDV")
        t0_chain(5)
        epilogue(4)
        aadd_mm = {}
        for dt in range(2):
            for jt in range(JT):
                pl.em("tensor",
                      lambda e, dt=dt, jt=jt:
                      e.matmul(ps_sh[:, dt, :],
                               amixb[:, jt, dt * P:(dt + 1) * P],
                               txb[:, jt, :], start=(jt == 0), stop=(jt == 1)),
                      w=([("sDV", txb_done), ("sAMX", AMX)]
                         if jt == 0 else ()),
                      inc=("sPE" if jt == 1 else None))
            aadd_mm[dt] = pl.v("sPE")
        t0_chain(6)
        epilogue(5)
        t0_chain(7)
        epilogue(6)
        t0_chain(0)
        epilogue(7)
        t0_chain(1)
        t0_last[q] = t0_mm_val[1]
        epilogue(0)
        epilogue(1)

    pl.vop(lambda e: e.reduce_sum(muN8[:, :], nacc[:, :, :],
                                  mybir.AxisListType.X))
    pl.vop(lambda e: e.tensor_scalar(muN8[:, :], muN8[:, :], 1.0 / D, None,
                                     ALU.mult))
    pl.vop(lambda e: e.reduce_sum(tstat[:, :], sqacc[:, :, :],
                                  mybir.AxisListType.X),
           w=[("sAC", max(sq_done.values()))])
    pl.vop(lambda e: e.tensor_scalar(tstat[:, :], tstat[:, :], 1.0 / D, None,
                                     ALU.mult))
    pl.vop(lambda e: e.tensor_tensor(rN8[:, :], muN8[:, :], muN8[:, :],
                                     ALU.mult))
    pl.vop(lambda e: e.tensor_tensor(tstat[:, :], tstat[:, :], rN8[:, :],
                                     ALU.subtract))
    nst = pl.v("sDV")
    pl.aop(lambda e: e.activation(tstat[:, :], tstat[:, :], AF.Ln,
                                  bias=c_eps), w=[("sDV", nst)])
    pl.aop(lambda e: e.activation(rN8[:, :], tstat[:, :], AF.Exp,
                                  scale=c_nh))
    rn_done = pl.v("sAC")
    pl.vop(lambda e: e.tensor_tensor(ngN8[:, :], muN8[:, :], rN8[:, :],
                                     ALU.mult), w=[("sAC", rn_done)])
    pl.vop(lambda e: e.tensor_scalar_mul(ngN8[:, :], ngN8[:, :], -1.0))
    ng_done = pl.v("sDV")
    last_tmpn = [("sAC", pl.v("sAC"))]
    for dt in range(DT):
        pl.vop(lambda e, dt=dt: e.scalar_tensor_tensor(
            xbuf[:, dt, :], attn_out[:, dt, :], rN8[:, dt:dt + 1],
            xbuf[:, dt, :], ALU.mult, ALU.add,
            accum_out=s1acc[:, dt:dt + 1]), w=[("sAC", rn_done)])
        x1_done[dt] = pl.v("sDV")
        if dt % 2 == 0:
            w_ = [("sDV", x1_done[dt])]
            if last_tmpn[0][0] == "sDV":
                w_.append(last_tmpn[0])
            pl.aop(lambda e, dt=dt: e.activation(
                tmpn[:, :], xbuf[:, dt, :], AF.Square,
                accum_out=s2acc[:, dt:dt + 1]), w=w_)
            last_tmpn[0] = ("sAC", pl.v("sAC"))
        else:
            w_ = [last_tmpn[0]] if last_tmpn[0][0] == "sAC" else []
            pl.vop(lambda e, dt=dt: e.scalar_tensor_tensor(
                tmpn[:, :], xbuf[:, dt, :], 1.0, xbuf[:, dt, :],
                ALU.mult, ALU.mult, accum_out=s2acc[:, dt:dt + 1]), w=w_)
            last_tmpn[0] = ("sDV", pl.v("sDV"))
        sp_done.append(pl.dma("sSP", "sync",
                              t_y.ap()[dt * P:(dt + 1) * P, :],
                              xbuf[:, dt, :], w=[("sDV", x1_done[dt])]))
    x1sq_ac = pl.v("sAC")

    pl.vop(lambda e: e.tensor_scalar(muF8[:, :], s1acc[:, :], 1.0 / D, None,
                                     ALU.mult))
    pl.vop(lambda e: e.tensor_scalar(lnt[:, :], s2acc[:, :], 1.0 / D, None,
                                     ALU.mult), w=[("sAC", x1sq_ac)])
    pl.vop(lambda e: e.tensor_tensor(rF[:, :], muF8[:, :], muF8[:, :],
                                     ALU.mult))
    pl.vop(lambda e: e.tensor_tensor(lnt[:, :], lnt[:, :], rF[:, :],
                                     ALU.subtract))
    fstat = pl.v("sDV")
    pl.aop(lambda e: e.activation(lnt[:, :], lnt[:, :], AF.Ln, bias=c_eps),
           w=[("sDV", fstat)])
    pl.aop(lambda e: e.activation(rF[:, :], lnt[:, :], AF.Exp, scale=c_nh))
    rf_done = pl.v("sAC")
    pl.vop(lambda e: e.tensor_tensor(ngF8[:, :], muF8[:, :], rF[:, :],
                                     ALU.mult), w=[("sAC", rf_done)])
    pl.vop(lambda e: e.tensor_scalar_mul(ngF8[:, :], ngF8[:, :], -1.0))
    ngf_done = pl.v("sDV")
    SP = pl.v("sSP")

    pl.emit(nc, sems)
    s1.close()
    s01.close()
    ps01.close()
    sT.close()

    sTl = ExitStack()
    lnfT = sTl.enter_context(nc.sbuf_tensor("lnfT", [P, HT, R], BF16,
                                            side="right"))
    sW2 = ExitStack()
    gw_sb = sW2.enter_context(nc.sbuf_tensor("gw_sb", [P, 4, D], BF16,
                                             side="right"))
    uw_sb = sW2.enter_context(nc.sbuf_tensor("uw_sb", [P, 4, D], BF16,
                                             side="right"))
    g_mm_done = {}
    u_mm_done = {}
    gw_ld = {}
    uw_ld = {}

    def load_chunk(ci):
        s0c = (ci % 2) * 2
        gsem = "sGW0" if ci % 2 == 0 else "sGW1"
        usem = "sUW0" if ci % 2 == 0 else "sUW1"
        wg = [("sPE", g_mm_done[2 * (ci - 2) + 1])] if ci >= 2 else []
        pl.dma(gsem, "gpsimd", gw_sb[:, s0c:s0c + 2, :],
               t_gw.ap()[2 * ci:2 * ci + 2].rearrange(
                   "ii p ht c -> p ii (ht c)"), w=wg)
        gw_ld[ci] = (gsem, pl.v(gsem))
        wu = [("sPE", u_mm_done[2 * (ci - 2) + 1])] if ci >= 2 else []
        pl.dma(usem, "gpsimd", uw_sb[:, s0c:s0c + 2, :],
               t_uw.ap()[2 * ci:2 * ci + 2].rearrange(
                   "ii p ht c -> p ii (ht c)"), w=wu)
        uw_ld[ci] = (usem, pl.v(usem))

    sTx = ExitStack()
    lnfbf = [sTx.enter_context(nc.sbuf_tensor(f"lnfbf{i}", [P, D], BF16))
             for i in range(2)]
    psT = ExitStack()
    ps_tr = [psT.enter_context(nc.psum_tensor(f"ps_tr{i}", [P, 4 * P], BF16))
             for i in range(2)]

    pl.new_block()
    pl.em("gpsimd", fn=None, w=[("sXTB", XTB)])
    load_chunk(0)
    load_chunk(1)
    tr_done = {}
    ev_done = {}
    for dt in range(DT):
        wcast = []
        if dt >= 2:
            wcast.append(("sPE", tr_done[dt - 2]))
        pl.aop(lambda e, dt=dt: e.activation(
            lnfbf[dt % 2][:, :], xbuf[:, dt, :], AF.Identity,
            scale=rF[:, dt:dt + 1], bias=ngF8[:, dt:dt + 1]), w=wcast)
        cast_done = pl.v("sAC")
        for g4 in range(4):
            pb = ps_tr[g4 % 2]
            wt = [("sAC", cast_done), ("sXI", XI)]
            prev = (dt, g4 - 2) if g4 >= 2 else ((dt - 1, g4 + 2)
                                                 if dt > 0 else None)
            if prev in ev_done:
                wt.append(("sDV", ev_done[prev]))
            for k in range(4):
                nt = g4 * 4 + k
                pl.em("tensor",
                      lambda e, dt=dt, nt=nt, k=k, pb=pb:
                      e.transpose(pb[:, k * P:(k + 1) * P],
                                  lnfbf[dt % 2][:, nt * P:(nt + 1) * P],
                                  ident_sb[:, :]),
                      w=(wt if k == 0 else ()),
                      inc=("sPE" if k == 3 else None))
            tg = pl.v("sPE")
            pl.vop(lambda e, dt=dt, g4=g4, pb=pb: e.tensor_copy(
                lnfT[:, g4 * 4:g4 * 4 + 4, dt * P:(dt + 1) * P],
                pb[:, :].rearrange("p (k c) -> p k c", c=P)),
                w=[("sPE", tg)])
            ev_done[(dt, g4)] = pl.v("sDV")
        tr_done[dt] = pl.v("sPE")

    pl.emit(nc, sems)
    sTx.close()
    sx.close()
    psT.close()

    s2 = ExitStack()
    acc = s2.enter_context(nc.sbuf_tensor("acc", [P, DT, D], F32))
    dsB = s2.enter_context(nc.sbuf_tensor("dsB", [P, D], F32))
    s2w = ExitStack()
    dwb = s2w.enter_context(nc.sbuf_tensor("dwb", [P, 8, D], BF16))
    hdn = s2w.enter_context(nc.sbuf_tensor("hdn", [P, 8, R], BF16))
    sg = s2w.enter_context(nc.sbuf_tensor("sg", [P, 2, R], BF16))
    w2b = s2w.enter_context(nc.sbuf_tensor("w2b", [P, 2, D], F32))
    ps2 = ExitStack()
    ps_f = ps2.enter_context(nc.psum_tensor("ps_f", [P, 8, MQ], F32))

    pl.new_block()

    pl.dma("sDS", "scalar", dsB[:, :], t_ds.ap(), w=[("sSP", SP)])
    dsb_done = ("sDS", pl.v("sDS"))

    dw_ld = {}
    silu_done = {}
    hdn_done = {}
    down_mm = {}
    accadd = {}
    dn_prev = {0: None, 4: None}
    out_dma = {}

    def emit_finale(tt):
        pl.vop(lambda e, tt=tt: e.scalar_tensor_tensor(
            acc[:, tt, :], acc[:, tt, :], 1.0, dsB[:, :], ALU.mult, ALU.mult,
            accum_out=wsum[:, tt:tt + 1]),
            w=[("sDV", accadd[(len(BLKS) - 1, tt)])])
        w1v = pl.v("sDV")
        pl.aop(lambda e, tt=tt: e.activation(
            w2b[:, tt % 2, :], acc[:, tt, :], AF.Square,
            accum_out=wsq[:, tt:tt + 1]),
            w=[("sDV", w1v)] + ([out_dma[tt - 2]] if tt - 2 in out_dma
                                else []))
        pl.aop(lambda e, tt=tt: e.mul(t1c[:, :], wsum[:, tt:tt + 1], 1.0 / D))
        pl.aop(lambda e, tt=tt: e.mul(t2c[:, :], wsq[:, tt:tt + 1], 1.0 / D))
        pl.aop(lambda e: e.square(t3c[:, :], t1c[:, :]))
        pl.aop(lambda e: e.activation(t3c[:, :], t3c[:, :], AF.Identity,
                                      scale=cst[:, 2:3], bias=t2c[:, :]))
        pl.aop(lambda e, tt=tt: e.activation(
            t3c[:, :], t3c[:, :], AF.Ln, bias=c_eps))
        pl.aop(lambda e, tt=tt: e.activation(
            rW8[:, tt:tt + 1], t3c[:, :], AF.Exp, scale=c_nh))
        pl.aop(lambda e, tt=tt: e.activation(
            t3c[:, :], rW8[:, tt:tt + 1], AF.Identity, scale=t1c[:, :]))
        pl.aop(lambda e, tt=tt: e.activation(
            ngW8[:, tt:tt + 1], t3c[:, :], AF.Identity, scale=cst[:, 2:3],
            bias=ngN8[:, tt:tt + 1]))
        pl.aop(lambda e, tt=tt: e.activation(
            w2b[:, tt % 2, :], acc[:, tt, :], AF.Identity,
            scale=rW8[:, tt:tt + 1], bias=ngW8[:, tt:tt + 1]))
        w2v = pl.v("sAC")
        osem = "sOUT" if tt % 2 == 0 else "sRL"
        pl.dma(osem, "gpsimd", t_y.ap()[tt * P:(tt + 1) * P, :],
               w2b[:, tt % 2, :], w=[("sAC", w2v), ("sSP", SP)], accum=True)
        out_dma[tt] = (osem, pl.v(osem))

    for bi, blk in enumerate(BLKS):
        ibase = sum(BLKS[:bi])
        ci0 = ibase // 2
        if bi > 0:
            load_chunk(ci0)
            if blk > 2:
                load_chunk(ci0 + 1)
        wdw = ([("sPE", down_mm[(bi - 1, 6)])] if bi > 0
               else [("sSP", SP)])
        for g in range(0, blk, 4):
            pl.dma("sDW", "gpsimd", dwb[:, g:g + 4, :],
                   t_dw.ap()[(ibase + g) * P:(ibase + g + 4) * P, :].rearrange(
                       "(li p) h -> p li h", p=P),
                   w=(wdw if g == 0 else ()))
        dw_ld[bi] = pl.v("sDW")

        for li in range(blk):
            gi = ibase + li
            if gi % 2 == 0 and gi // 2 > ci0 + 1:
                load_chunk(gi // 2)
            slot = (gi // 2 % 2) * 2 + gi % 2
            wg = [gw_ld[gi // 2]]
            if gi >= 1:
                wg.append(("sAC", silu_done[gi - 1]))
            if li == 0 and bi > 0:
                wg.append(("sDV", accadd[(bi - 1, 7)]))
            for half in range(2):
                for ht in range(HT):
                    pl.em("tensor",
                          lambda e, slot=slot, half=half, ht=ht:
                          e.matmul(ps_f[:, half, :],
                                   gw_sb[:, slot, ht * P:(ht + 1) * P],
                                   lnfT[:, ht, half * 512:(half + 1) * 512],
                                   start=(ht == 0), stop=(ht == HT - 1)),
                          w=(wg if (half == 0 and ht == 0) else ()),
                          inc=("sPE" if (half == 1 and ht == HT - 1) else None))
            g_mm_done[gi] = pl.v("sPE")
            wu = [uw_ld[gi // 2]]
            if gi >= 1:
                wu.append(("sDV", hdn_done[gi - 1]))
            if li == 0 and bi > 0:
                wu.append(("sDV", accadd[(bi - 1, 7)]))
            for half in range(2):
                for ht in range(HT):
                    pl.em("tensor",
                          lambda e, slot=slot, half=half, ht=ht:
                          e.matmul(ps_f[:, 2 + half, :],
                                   uw_sb[:, slot, ht * P:(ht + 1) * P],
                                   lnfT[:, ht, half * 512:(half + 1) * 512],
                                   start=(ht == 0), stop=(ht == HT - 1)),
                          w=(wu if (half == 0 and ht == 0) else ()),
                          inc=("sPE" if (half == 1 and ht == HT - 1) else None))
            u_mm_done[gi] = pl.v("sPE")
            ws = [("sPE", g_mm_done[gi]), ("sSM", SM)]
            if gi >= 2:
                ws.append(("sDV", hdn_done[gi - 2]))
            pl.aop(lambda e, gi=gi: e.activation(
                sg[:, gi % 2, :], ps_f[:, 0:2, :], AF.Silu,
                scale=gs_sb[:, gi:gi + 1]), w=ws)
            silu_done[gi] = pl.v("sAC")
            pl.vop(lambda e, gi=gi, li=li: e.scalar_tensor_tensor(
                hdn[:, li, :], ps_f[:, 2:4, :], us_sb[:, gi:gi + 1],
                sg[:, gi % 2, :], ALU.mult, ALU.mult),
                w=[("sPE", u_mm_done[gi]), ("sAC", silu_done[gi])])
            hdn_done[gi] = pl.v("sDV")

        DSEQ = [1, 0, 3, 2, 5, 4, 7, 6]
        for si, tt in enumerate(DSEQ):
            bs = 4 if tt % 2 == 0 else 0
            wdn = [("sDV", hdn_done[ibase + blk - 1]), ("sDW", dw_ld[bi])]
            prev = dn_prev[bs]
            dn_prev[bs] = (bi, tt)
            if prev in accadd:
                wdn.append(("sDV", accadd[prev]))
            if bi == 0 and tt == 0:
                wdn.append(dsb_done)
            if tt == 1:
                wdn.append(("sAC", silu_done[ibase + blk - 1]))
            for li in range(blk):
                for hc in range(4):
                    pl.em("tensor",
                          lambda e, li=li, hc=hc, tt=tt, bs=bs, blk=blk:
                          e.matmul(ps_f[:, bs + hc, :],
                                   hdn[:, li, tt * P:(tt + 1) * P],
                                   dwb[:, li, hc * 512:(hc + 1) * 512],
                                   start=(li == 0), stop=(li == blk - 1)),
                          w=(wdn if (li == 0 and hc == 0) else ()),
                          inc=("sPE" if (li == blk - 1 and hc == 3) else None))
            down_mm[(bi, tt)] = pl.v("sPE")
            if bi == 0:
                pl.vop(lambda e, tt=tt, bs=bs: e.tensor_copy(
                    acc[:, tt, :].rearrange("p (a b) -> p a b", b=512),
                    ps_f[:, bs:bs + 4, :]),
                    w=[("sPE", down_mm[(bi, tt)])])
            else:
                pl.vop(lambda e, tt=tt, bs=bs: e.tensor_tensor(
                    acc[:, tt, :].rearrange("p (a b) -> p a b", b=512),
                    acc[:, tt, :].rearrange("p (a b) -> p a b", b=512),
                    ps_f[:, bs:bs + 4, :], ALU.add),
                    w=[("sPE", down_mm[(bi, tt)])])
            accadd[(bi, tt)] = pl.v("sDV")
            if bi == len(BLKS) - 1 and si >= 1:
                emit_finale(DSEQ[si - 1])

    emit_finale(DSEQ[-1])
    pl.em("sync", fn=None, w=[out_dma[6], out_dma[7]])

    pl.emit(nc, sems)
    s2w.close()
    sW2.close()
    sTl.close()


    s2.close()
    ps2.close()
    glob.close()
    es.close()
    return nc


def kernel(L0_attn_Z_in, L0_attn_A_l, L0_attn_r_l, L0_attn_B_l, L0_attn_C_l,
           L0_ffn_gate_w, L0_ffn_up_w, L0_ffn_down_w, L0_ffn_gate_s,
           L0_ffn_up_s, L0_ffn_down_s):
    f32 = np.float32
    Z = np.asarray(L0_attn_Z_in, dtype=f32)
    A_l = np.asarray(L0_attn_A_l, dtype=f32)
    r_l = np.asarray(L0_attn_r_l, dtype=f32)
    B_l = np.asarray(L0_attn_B_l, dtype=f32)
    C_l = np.asarray(L0_attn_C_l, dtype=f32)
    gate_w = np.asarray(L0_ffn_gate_w, dtype=f32)
    up_w = np.asarray(L0_ffn_up_w, dtype=f32)
    down_w = np.asarray(L0_ffn_down_w, dtype=f32)
    gate_s = np.asarray(L0_ffn_gate_s, dtype=f32)
    up_s = np.asarray(L0_ffn_up_s, dtype=f32)
    down_s = np.asarray(L0_ffn_down_s, dtype=f32)

    def pack_w(w):
        return np.ascontiguousarray(
            w.T.reshape(HT, P, IT, P).transpose(2, 1, 0, 3))

    gw = pack_w(gate_w)
    uw = pack_w(up_w)
    dw = np.ascontiguousarray(down_w.T)
    gs = np.ascontiguousarray(gate_s.reshape(IT, P).T)
    us = np.ascontiguousarray(up_s.reshape(IT, P).T)
    ds = np.ascontiguousarray(np.broadcast_to(down_s.reshape(1, D), (P, D)))
    blT = np.ascontiguousarray(B_l.T)
    clT = np.ascontiguousarray(C_l.T)
    rl = np.ascontiguousarray(np.broadcast_to(r_l.reshape(1, 1), (P, 1)))
    lastm = np.ones((P, 1), f32)
    lastm[P - 1, 0] = 0.0
    ident = np.eye(P, dtype=np.float32).astype(ml_dtypes.bfloat16)

    nc = build_nc()

    in_maps = []
    for c in range(8):
        b, h = c // 2, c % 2
        xr = np.ascontiguousarray(Z[b, h * R:(h + 1) * R, :])
        xT = np.ascontiguousarray(xr.T)
        xtop = np.ascontiguousarray(Z[b, :J, :])
        xtopT = np.ascontiguousarray(xtop.T)
        amixT = np.zeros((J, R), f32)
        maskc = np.ones((P, DT), f32)
        if h == 0:
            amixT[:, :J] = A_l.T
            maskc[:, 0:JT] = 0.0
        in_maps.append({
            "x": xr, "xT": xT, "xtop": xtop, "xtopT": xtopT,
            "blT": blT, "clT": clT, "amixT": amixT,
            "rl": rl, "maskc": maskc, "lastm": lastm, "ident": ident,
            "gw": gw, "uw": uw, "dw": dw,
            "gsc": gs, "usc": us, "dsr": ds,
        })

    res = run_bass_kernel_spmd(nc, in_maps, core_ids=list(range(8)))
    out = np.empty((4, D, D), f32)
    for c in range(8):
        b, h = c // 2, c % 2
        out[b, h * R:(h + 1) * R, :] = res.results[c]["y"]
    return out


# version 20
# speedup vs baseline: 1.0100x; 1.0100x over previous
from contextlib import ExitStack

import numpy as np
import ml_dtypes

import concourse.bass as bass
import concourse.mybir as mybir
from concourse.bass_utils import run_bass_kernel_spmd

F32 = mybir.dt.float32
BF16 = mybir.dt.bfloat16
AF = mybir.ActivationFunctionType
ALU = mybir.AluOpType

P = 128
D = 2048
R = 1024
DT = R // P
NT = D // P
MQ = 512
NQ = D // MQ
J = 256
JT = J // P
II = 5632
IT = II // P
HT = D // P
EPS = 1e-5
BLKS = [8, 8, 8, 8, 8, 4]

ENGS = ("sync", "gpsimd", "scalar", "vector", "tensor")
SEMS = ("sXI", "sXT", "sSM", "sSP", "sRL", "sRL2", "sOUT", "sXTB", "sXTT",
        "sWB0", "sAMX", "sDS", "sGW0", "sGW1", "sUW0", "sUW1", "sDW", "sPE",
        "sDV", "sAC", "sGP")


class Plan:

    def __init__(self):
        self.cnt = {s: 0 for s in SEMS}
        self.ops = None
        self.last_wait = {e: {} for e in ENGS}

    def new_block(self):
        self.ops = {e: [] for e in ENGS}

    def em(self, eng, fn=None, w=(), inc=None, n=1):
        self.ops[eng].append((tuple(w), fn, inc, n))
        if inc is not None:
            self.cnt[inc] += n
            return self.cnt[inc]
        return None

    def gop(self, fn, w=()):
        return self.em("gpsimd", fn, w=list(w) + [("sGP", self.cnt["sGP"])],
                       inc="sGP")

    def vop(self, fn, w=()):
        return self.em("vector", fn, w=list(w) + [("sDV", self.cnt["sDV"])],
                       inc="sDV")

    def aop(self, fn, w=()):
        return self.em("scalar", fn, w=list(w) + [("sAC", self.cnt["sAC"])],
                       inc="sAC")

    def dma(self, sem, q, out, in_, w=(), accum=False):
        if accum:
            return self.em(q, lambda e, o=out, i=in_: e.dma_start(
                o, i, accum_op=ALU.add), w=w, inc=sem, n=16)
        return self.em(q, lambda e, o=out, i=in_: e.dma_start(o, i), w=w,
                       inc=sem, n=16)

    def v(self, s):
        return self.cnt[s]

    def emit(self, nc, sems):
        ops = self.ops
        with nc.Block() as blk:
            for ename in ENGS:
                if not ops[ename]:
                    continue

                def mk(lst=ops[ename], ename=ename):
                    def f(eng):
                        lw = self.last_wait[ename]
                        for w, fn, inc, n in lst:
                            for s, val in w:
                                if val > lw.get(s, 0):
                                    eng.wait_ge(sems[s], val)
                                    lw[s] = val
                            if fn is not None:
                                ins = fn(eng)
                                if inc is not None:
                                    ins.then_inc(sems[inc], n)
                    return f

                getattr(blk, ename)(mk())


def build_nc():
    nc = bass.Bass("TRN2", target_bir_lowering=False, debug=False)

    t_x = nc.dram_tensor("x", [R, D], F32, kind="ExternalInput")
    t_xT = nc.dram_tensor("xT", [D, R], F32, kind="ExternalInput")
    t_xtop = nc.dram_tensor("xtop", [J, D], F32, kind="ExternalInput")
    t_xtopT = nc.dram_tensor("xtopT", [D, J], F32, kind="ExternalInput")
    t_blT = nc.dram_tensor("blT", [J, J], F32, kind="ExternalInput")
    t_clT = nc.dram_tensor("clT", [J, J], F32, kind="ExternalInput")
    t_amixT = nc.dram_tensor("amixT", [J, R], F32, kind="ExternalInput")
    t_rl = nc.dram_tensor("rl", [P, 1], F32, kind="ExternalInput")
    t_mask = nc.dram_tensor("maskc", [P, DT], F32, kind="ExternalInput")
    t_lastm = nc.dram_tensor("lastm", [P, 1], F32, kind="ExternalInput")
    t_ident = nc.dram_tensor("ident", [P, P], BF16, kind="ExternalInput")
    t_gw = nc.dram_tensor("gw", [IT, P, HT, P], F32, kind="ExternalInput")
    t_uw = nc.dram_tensor("uw", [IT, P, HT, P], F32, kind="ExternalInput")
    t_dw = nc.dram_tensor("dw", [II, D], F32, kind="ExternalInput")
    t_gs = nc.dram_tensor("gsc", [P, IT], F32, kind="ExternalInput")
    t_us = nc.dram_tensor("usc", [P, IT], F32, kind="ExternalInput")
    t_ds = nc.dram_tensor("dsr", [P, D], F32, kind="ExternalInput")
    t_y = nc.dram_tensor("y", [R, D], F32, kind="ExternalOutput")

    es = ExitStack()
    sems = {s: es.enter_context(nc.semaphore(s)) for s in SEMS}
    pl = Plan()

    glob = ExitStack()

    def g_sb(name, shape, dtype=F32):
        return glob.enter_context(nc.sbuf_tensor(name, shape, dtype))

    muvar1 = g_sb("muvar1", [P, DT, 2])
    r1 = g_sb("r1", [P, DT])
    A2 = g_sb("A2", [P, DT])
    A3 = g_sb("A3", [P, DT])
    A4 = g_sb("A4", [P, DT])
    srow = g_sb("srow", [P, DT])
    rl_sb = g_sb("rl_sb", [P, 1])
    mask_sb = g_sb("mask_sb", [P, DT])
    lm_sb = g_sb("lm_sb", [P, 1])
    Xmv = g_sb("Xmv", [P, JT, 2])
    Xr = g_sb("Xr", [P, JT])
    nrmuX = g_sb("nrmuX", [P, JT])
    bn6 = g_sb("bn6", [P, 4, 6])
    lnt = g_sb("lnt", [P, DT])
    sc2 = g_sb("sc2", [P, JT])
    muvF = g_sb("muvF", [P, DT, 2])
    rF = g_sb("rF", [P, DT])
    muvN = g_sb("muvN", [P, 2])
    rN = g_sb("rN", [P, 1])
    ones_c = g_sb("ones_c", [P, 1], BF16)
    ones_r = g_sb("ones_r", [1, P], BF16)
    gs_sb = g_sb("gs_sb", [P, IT])
    us_sb = g_sb("us_sb", [P, IT])
    ident_sb = g_sb("ident_sb", [P, P], BF16)
    muvW = g_sb("muvW", [P, 2])
    rW = g_sb("rW", [P, 1])
    nacc = g_sb("nacc", [P, DT, NQ])
    sqacc = g_sb("sqacc", [P, DT, NQ])
    s1acc = g_sb("s1acc", [P, DT])
    s2acc = g_sb("s2acc", [P, DT])
    muN8 = g_sb("muN8", [P, DT])
    rN8 = g_sb("rN8", [P, DT])
    ngN8 = g_sb("ngN8", [P, DT])
    muF8 = g_sb("muF8", [P, DT])
    ngF8 = g_sb("ngF8", [P, DT])
    t1c = g_sb("t1c", [P, 1])
    t3c = g_sb("t3c", [P, 1])
    tstat = g_sb("tstat", [P, DT])
    t2c = g_sb("t2c", [P, 1])
    wsum = g_sb("wsum", [P, DT])
    wsq = g_sb("wsq", [P, DT])
    ngW = g_sb("ngW", [P, 1])
    nA2 = g_sb("nA2", [P, DT])
    muW8 = g_sb("muW8", [P, DT])
    rW8 = g_sb("rW8", [P, DT])
    ngW8 = g_sb("ngW8", [P, DT])
    cst = g_sb("cst", [P, 3])
    c_eps = cst[:, 0:1]
    c_nh = cst[:, 1:2]

    sx = ExitStack()
    xbuf = sx.enter_context(nc.sbuf_tensor("xbuf", [P, DT, D], F32))
    s01 = ExitStack()
    xTb = s01.enter_context(nc.sbuf_tensor("xTb", [P, NT, R], BF16, side="right"))
    ub = s01.enter_context(nc.sbuf_tensor("ub", [P, JT, D], BF16, side="right"))
    wb = s01.enter_context(nc.sbuf_tensor("wb", [P, JT, D], BF16, side="right"))
    xtopTb = s01.enter_context(
        nc.sbuf_tensor("xtopTb", [P, NT, J], BF16, side="right"))
    amixb = s01.enter_context(
        nc.sbuf_tensor("amixb", [P, JT, R], BF16, side="right"))
    sT = ExitStack()
    attn_out = sT.enter_context(nc.sbuf_tensor("attn_out", [P, DT, D], BF16))
    s0 = ExitStack()
    xtop_sb = s0.enter_context(nc.sbuf_tensor("xtop_sb", [P, JT, D], F32))
    Xn = s0.enter_context(nc.sbuf_tensor("Xn", [P, JT, D], BF16))
    bltb = s0.enter_context(nc.sbuf_tensor("bltb", [P, JT, J], BF16))
    cltb = s0.enter_context(nc.sbuf_tensor("cltb", [P, JT, J], BF16))
    blclf = s0.enter_context(nc.sbuf_tensor("blclf", [P, 2 * JT, J], F32))
    ps01 = ExitStack()
    ps_a = ps01.enter_context(nc.psum_tensor("ps_a", [P, 2, MQ], F32))
    ps_t = ps01.enter_context(nc.psum_tensor("ps_t", [P, 3, MQ], F32))
    ps_sh = ps01.enter_context(nc.psum_tensor("ps_sh", [P, 2, MQ], F32))

    def ln_stats(src2048, muvar_out, w_extra=()):
        for c in range(4):
            pl.vop(lambda e, c=c, s=src2048: e.bn_stats(
                bn6[:, c, :], s[:, c * 512:(c + 1) * 512]), w=w_extra)
        pl.vop(lambda e, m=muvar_out: e.bn_aggr(
            m, bn6[:, :, :].rearrange("p a b -> p (a b)")))

    pl.new_block()

    pl.dma("sXI", "sync", ident_sb[:, :], t_ident.ap())

    pl.dma("sXT", "scalar", xtop_sb[:, 0, :], t_xtop.ap()[0:P, :])
    pl.dma("sXT", "sync", xtop_sb[:, 1, :], t_xtop.ap()[P:2 * P, :])
    XT = pl.v("sXT")
    for dt in range(DT):
        pl.dma("sXI", "sync", xbuf[:, dt, :],
               t_x.ap()[dt * P:(dt + 1) * P, :])
    XI = pl.v("sXI")
    for jt in range(JT):
        pl.dma("sWB0", "scalar", blclf[:, jt, :],
               t_blT.ap()[jt * P:(jt + 1) * P, :])
        pl.dma("sWB0", "scalar", blclf[:, JT + jt, :],
               t_clT.ap()[jt * P:(jt + 1) * P, :])
    WB0 = pl.v("sWB0")
    pl.dma("sSM", "scalar", rl_sb[:, :], t_rl.ap())
    pl.dma("sSM", "scalar", mask_sb[:, :], t_mask.ap())
    pl.dma("sSM", "scalar", lm_sb[:, :], t_lastm.ap())
    pl.dma("sSM", "scalar", gs_sb[:, :], t_gs.ap())
    pl.dma("sSM", "scalar", us_sb[:, :], t_us.ap())
    SM = pl.v("sSM")

    for k in range(2):
        pl.dma("sXTT", "gpsimd", xtopTb[:, 8 * k:8 * k + 8, :],
               t_xtopT.ap()[8 * k * P:8 * (k + 1) * P, :].rearrange(
                   "(nt p) j -> p nt j", p=P))
    XTT = pl.v("sXTT")
    pl.dma("sAMX", "gpsimd", amixb[:, :, :],
           t_amixT.ap().rearrange("(jt p) d -> p jt d", p=P))
    AMX = pl.v("sAMX")
    for k in range(4):
        pl.dma("sXTB", "gpsimd", xTb[:, 4 * k:4 * k + 4, :],
               t_xT.ap()[4 * k * P:4 * (k + 1) * P, :].rearrange(
                   "(nt p) d -> p nt d", p=P))
    XTB = pl.v("sXTB")

    pl.vop(lambda e: e.memset(cst[:, 0:1], EPS))
    pl.vop(lambda e: e.memset(cst[:, 1:2], -0.5))
    pl.vop(lambda e: e.memset(cst[:, 2:3], -1.0))
    cstv = pl.v("sDV")
    pl.vop(lambda e: e.memset(ones_c[:, :], 1.0))
    pl.vop(lambda e: e.memset(ones_r[:, :], 1.0))

    pl.aop(lambda e: e.activation(sc2[:, 0:1], cst[:, 0:1], AF.Ln,
                                  bias=c_eps), w=[("sDV", cstv)])
    pl.aop(lambda e: e.activation(sc2[:, 0:1], sc2[:, 0:1], AF.Exp,
                                  scale=c_nh))

    for jt in range(JT):
        pl.aop(lambda e, jt=jt: e.activation(bltb[:, jt, :], blclf[:, jt, :],
                                             AF.Copy),
               w=([("sWB0", WB0)] if jt == 0 else []))
        pl.aop(lambda e, jt=jt: e.activation(cltb[:, jt, :],
                                             blclf[:, JT + jt, :], AF.Copy))
    blcl_cast = pl.v("sAC")

    for jt in range(JT):
        ln_stats(xtop_sb[:, jt, :], Xmv[:, jt, :], w_extra=[("sXT", XT)])
    xstat = pl.v("sDV")
    pl.aop(lambda e: e.activation(sc2[:, :], Xmv[:, :, 1], AF.Ln, bias=c_eps),
           w=[("sDV", xstat)])
    pl.aop(lambda e: e.activation(Xr[:, :], sc2[:, :], AF.Exp, scale=c_nh))
    xr_done = pl.v("sAC")
    for jt in range(JT):
        pl.vop(lambda e, jt=jt: e.tensor_scalar(
            Xn[:, jt, :], xtop_sb[:, jt, :],
            Xmv[:, jt, 0:1], Xr[:, jt:jt + 1], ALU.subtract, ALU.mult),
            w=[("sAC", xr_done)])
    xn_done = pl.v("sDV")
    pl.vop(lambda e: e.tensor_tensor(nrmuX[:, :], Xmv[:, :, 0], Xr[:, :],
                                     ALU.mult))
    pl.vop(lambda e: e.tensor_scalar_mul(nrmuX[:, :], nrmuX[:, :], -1.0))

    uw_bank_free = {}
    for (wsb, dst) in ((bltb, ub), (cltb, wb)):
        for it2 in range(JT):
            for c4 in range(4):
                bk = (it2 * 4 + c4) % 2
                w = [("sAC", blcl_cast), ("sDV", xn_done)]
                if bk in uw_bank_free:
                    w.append(("sAC", uw_bank_free[bk]))
                for jt in range(JT):
                    pl.em("tensor",
                          lambda e, wsb=wsb, it2=it2, c4=c4, jt=jt, bk=bk:
                          e.matmul(ps_a[:, bk, :],
                                   wsb[:, jt, it2 * P:(it2 + 1) * P],
                                   Xn[:, jt, c4 * 512:(c4 + 1) * 512],
                                   start=(jt == 0), stop=(jt == 1)),
                          w=(w if jt == 0 else ()),
                          inc=("sPE" if jt == 1 else None))
                mm = pl.v("sPE")
                pl.aop(lambda e, dst=dst, it2=it2, c4=c4, bk=bk:
                       e.activation(dst[:, it2, c4 * 512:(c4 + 1) * 512],
                                    ps_a[:, bk, :], AF.Copy),
                       w=[("sPE", mm)])
                uw_bank_free[bk] = pl.v("sAC")
    uw_done_pe = pl.v("sPE")
    uw_done_ac = pl.v("sAC")


    pl.emit(nc, sems)
    s0.close()

    s1 = ExitStack()
    att2 = s1.enter_context(nc.sbuf_tensor("att2", [P, 2, NT, MQ], BF16))
    colsumB = s1.enter_context(nc.sbuf_tensor("colsumB", [P, MQ], F32))
    cs_row = s1.enter_context(nc.sbuf_tensor("cs_row", [1, MQ], BF16))
    txb = s1.enter_context(nc.sbuf_tensor("txb", [P, JT, MQ], BF16))
    txf = s1.enter_context(nc.sbuf_tensor("txf", [P, MQ], F32))
    tmpe = s1.enter_context(nc.sbuf_tensor("tmpe", [P, 2, MQ], F32))
    tmpe2 = s1.enter_context(nc.sbuf_tensor("tmpe2", [P, MQ], F32))
    tmpn = s1.enter_context(nc.sbuf_tensor("tmpn", [P, D], BF16))

    pl.new_block()

    for dt in range(DT):
        ln_stats(xbuf[:, dt, :], muvar1[:, dt, :], w_extra=[("sXI", XI)])
    ln1_done = pl.v("sDV")
    coeffs_done = [0]

    def emit_r1_coeffs():
        pl.aop(lambda e: e.activation(lnt[:, :], muvar1[:, :, 1], AF.Ln,
                                      bias=c_eps), w=[("sDV", ln1_done)])
        pl.aop(lambda e: e.activation(r1[:, :], lnt[:, :], AF.Exp,
                                      scale=c_nh))
        r1_done = pl.v("sAC")
        pl.vop(lambda e: e.tensor_tensor(A2[:, :], muvar1[:, :, 0], r1[:, :],
                                         ALU.mult), w=[("sAC", r1_done)])
        pl.vop(lambda e: e.tensor_scalar(srow[:, :], mask_sb[:, :],
                                         rl_sb[:, 0:1], None, ALU.mult),
               w=[("sSM", SM)])
        pl.vop(lambda e: e.tensor_tensor(A3[:, :], srow[:, :], r1[:, :],
                                         ALU.mult))
        pl.vop(lambda e: e.tensor_tensor(A4[:, :], A2[:, :], srow[:, :],
                                         ALU.mult))
        pl.vop(lambda e: e.tensor_scalar_mul(A4[:, :], A4[:, :], -1.0))
        pl.vop(lambda e: e.tensor_scalar_mul(nA2[:, :], A2[:, :], -1.0))
        coeffs_done[0] = pl.v("sDV")

    relu_cnt = {}
    t0_last = {}
    sh_free = {}
    pst_free = {}
    epi_done = {}
    epi_v2 = {0: 0, 1: 0}
    epi_i = [0]
    sq_done = {}
    sp_done = []
    x1_done = {}
    x1sq_done = {}

    def emit_x1_chain(dt):
        qp = NQ - 1
        qsp = slice(qp * MQ, (qp + 1) * MQ)
        pl.aop(lambda e, dt=dt, qp=qp, qsp=qsp: e.activation(
            tmpn[:, 0:MQ], attn_out[:, dt, qsp], AF.Square,
            accum_out=sqacc[:, dt, qp:qp + 1]),
            w=[("sDV", epi_done[(qp, dt)])])
        sq_done[(qp, dt)] = pl.v("sAC")

    for q in range(NQ):
        qb = q % 2
        qs = slice(q * MQ, (q + 1) * MQ)
        for nt in range(NT):
            bk = nt % 2
            w = []
            if (q, nt - 2) in relu_cnt:
                w.append(relu_cnt[(q, nt - 2)])
            elif q > 0:
                w.append(relu_cnt[(q - 1, NT - 2 + nt)])
            else:
                w.append(("sAC", uw_done_ac))
            for dd in range(JT):
                pl.em("tensor",
                      lambda e, nt=nt, dd=dd, bk=bk, qs=qs:
                      e.matmul(ps_a[:, bk, :], ub[:, dd, nt * P:(nt + 1) * P],
                               wb[:, dd, qs], start=(dd == 0), stop=(dd == 1)),
                      w=(w if dd == 0 else ()),
                      inc=("sPE" if dd == 1 else None))
            mm = pl.v("sPE")
            wr = [("sPE", mm)]
            if q >= 2:
                wr.append(("sPE", t0_last[q - 2]))
            if nt % 2 == 0 or q == 0:
                pl.aop(lambda e, nt=nt, qb=qb, bk=bk:
                       e.activation(att2[:, qb, nt, :], ps_a[:, bk, :],
                                    AF.Relu), w=wr)
                relu_cnt[(q, nt)] = ("sAC", pl.v("sAC"))
            else:
                pl.vop(lambda e, nt=nt, qb=qb, bk=bk:
                       e.tensor_scalar_max(att2[:, qb, nt, :],
                                           ps_a[:, bk, :], 0.0), w=wr)
                relu_cnt[(q, nt)] = ("sDV", pl.v("sDV"))
        att_fix = 0
        if q == NQ - 1:
            pl.vop(lambda e, qb=qb: e.tensor_scalar(
                att2[:, qb, NT - 1, MQ - 1:MQ],
                att2[:, qb, NT - 1, MQ - 1:MQ],
                lm_sb[:, 0:1], None, ALU.mult),
                w=[relu_cnt[(q, NT - 1)], relu_cnt[(q, NT - 2)], ("sSM", SM)])
            att_fix = pl.v("sDV")

        w0 = [relu_cnt[(q, NT - 1)], relu_cnt[(q, NT - 2)]]
        if att_fix:
            w0.append(("sDV", att_fix))
        if 0 in sh_free:
            w0.append(sh_free[0])
        for nt in range(NT):
            pl.em("tensor",
                  lambda e, nt=nt, qb=qb:
                  e.matmul(ps_sh[0:1, 0, :], ones_c[:, :], att2[:, qb, nt, :],
                           start=(nt == 0), stop=(nt == NT - 1)),
                  w=(w0 if nt == 0 else ()),
                  inc=("sPE" if nt == NT - 1 else None))
        cs_mm = pl.v("sPE")
        pl.aop(lambda e: e.activation(cs_row[:, :], ps_sh[0:1, 0, :], AF.Copy),
               w=[("sPE", cs_mm)])
        cs_ev = pl.v("sAC")
        if q == 0:
            emit_r1_coeffs()

        t0_mm_val = {}

        def t0_chain(dt):
            bidx = dt % 3
            wt0 = [("sXTB", XTB)]
            if bidx in pst_free:
                wt0.append(("sDV", pst_free[bidx]))
            for nt in range(NT):
                pl.em("tensor",
                      lambda e, dt=dt, nt=nt, qb=qb, bidx=bidx:
                      e.matmul(ps_t[:, bidx, :],
                               xTb[:, nt, dt * P:(dt + 1) * P],
                               att2[:, qb, nt, :],
                               start=(nt == 0), stop=(nt == NT - 1)),
                      w=(wt0 if nt == 0 else ()),
                      inc=("sPE" if nt == NT - 1 else None))
            t0_mm_val[dt] = pl.v("sPE")

        def epilogue(dt):
            bidx = dt % 3
            sl = epi_i[0] % 2
            epi_i[0] += 1
            pl.aop(lambda e, dt=dt, qs=qs, sl=sl: e.activation(
                tmpe[:, sl, :], xbuf[:, dt, qs], AF.Identity,
                scale=r1[:, dt:dt + 1], bias=nA2[:, dt:dt + 1]),
                w=[("sDV", max(epi_v2[sl], coeffs_done[0]))])
            a1 = pl.v("sAC")
            pl.vop(lambda e, dt=dt, sl=sl: e.scalar_tensor_tensor(
                tmpe2[:, :], colsumB[:, :], A4[:, dt:dt + 1], tmpe[:, sl, :],
                ALU.mult, ALU.add), w=[("sAC", a1)])
            epi_v2[sl] = pl.v("sDV")
            if dt < 2:
                pl.vop(lambda e, dt=dt: e.tensor_tensor(
                    tmpe2[:, :], tmpe2[:, :], ps_sh[:, dt, :], ALU.add),
                    w=[("sPE", aadd_mm[dt])])
                sh_free[dt] = ("sDV", pl.v("sDV"))
            pl.vop(lambda e, dt=dt, qs=qs, bidx=bidx, q=q:
                e.scalar_tensor_tensor(
                    attn_out[:, dt, qs], ps_t[:, bidx, :], A3[:, dt:dt + 1],
                    tmpe2[:, :], ALU.mult, ALU.add,
                    accum_out=nacc[:, dt, q:q + 1]),
                w=[("sPE", t0_mm_val[dt])])
            pst_free[bidx] = pl.v("sDV")
            epi_done[(q, dt)] = pl.v("sDV")
            if q >= 1:
                qp = q - 1
                qsp = slice(qp * MQ, (qp + 1) * MQ)
                pl.aop(lambda e, dt=dt, qp=qp, qsp=qsp: e.activation(
                    tmpn[:, 0:MQ], attn_out[:, dt, qsp], AF.Square,
                    accum_out=sqacc[:, dt, qp:qp + 1]),
                    w=[("sDV", epi_done[(qp, dt)])])
                sq_done[(qp, dt)] = pl.v("sAC")
            if q == NQ - 1:
                emit_x1_chain(dt)

        def tx_chain(jt):
            wtx = ([("sAC", cs_ev)] if jt == 0 else [("sDV", csb_done)])
            wtx.append(("sXTT", XTT))
            for nt in range(NT):
                pl.em("tensor",
                      lambda e, jt=jt, nt=nt, qb=qb:
                      e.matmul(ps_sh[:, jt, :],
                               xtopTb[:, nt, jt * P:(jt + 1) * P],
                               att2[:, qb, nt, :],
                               start=(nt == 0), stop=(nt == NT - 1)),
                      w=(wtx if nt == 0 else ()),
                      inc=("sPE" if nt == NT - 1 else None))
            tx_mm = pl.v("sPE")
            pl.vop(lambda e, jt=jt: e.tensor_scalar(
                txf[:, :], ps_sh[:, jt, :], Xr[:, jt:jt + 1], None, ALU.mult),
                w=[("sPE", tx_mm)])
            pl.vop(lambda e, jt=jt: e.scalar_tensor_tensor(
                txb[:, jt, :], colsumB[:, :], nrmuX[:, jt:jt + 1], txf[:, :],
                ALU.mult, ALU.add))

        t0_chain(2)
        wb1 = [("sAC", cs_ev)]
        if 1 in sh_free:
            wb1.append(sh_free[1])
        pl.em("tensor",
              lambda e: e.matmul(ps_sh[:, 1, :], ones_r[:, :], cs_row[:, :],
                                 start=True, stop=True),
              w=wb1, inc="sPE")
        bc_mm = pl.v("sPE")
        t0_chain(3)
        pl.vop(lambda e: e.tensor_copy(colsumB[:, :], ps_sh[:, 1, :]),
               w=[("sPE", bc_mm)])
        csb_done = pl.v("sDV")
        epilogue(2)
        tx_chain(0)
        t0_chain(4)
        epilogue(3)
        tx_chain(1)
        txb_done = pl.v("sDV")
        t0_chain(5)
        epilogue(4)
        aadd_mm = {}
        for dt in range(2):
            for jt in range(JT):
                pl.em("tensor",
                      lambda e, dt=dt, jt=jt:
                      e.matmul(ps_sh[:, dt, :],
                               amixb[:, jt, dt * P:(dt + 1) * P],
                               txb[:, jt, :], start=(jt == 0), stop=(jt == 1)),
                      w=([("sDV", txb_done), ("sAMX", AMX)]
                         if jt == 0 else ()),
                      inc=("sPE" if jt == 1 else None))
            aadd_mm[dt] = pl.v("sPE")
        t0_chain(6)
        epilogue(5)
        t0_chain(7)
        epilogue(6)
        t0_chain(0)
        epilogue(7)
        t0_chain(1)
        t0_last[q] = t0_mm_val[1]
        epilogue(0)
        epilogue(1)

    pl.vop(lambda e: e.reduce_sum(muN8[:, :], nacc[:, :, :],
                                  mybir.AxisListType.X))
    pl.vop(lambda e: e.tensor_scalar(muN8[:, :], muN8[:, :], 1.0 / D, None,
                                     ALU.mult))
    pl.vop(lambda e: e.reduce_sum(tstat[:, :], sqacc[:, :, :],
                                  mybir.AxisListType.X),
           w=[("sAC", max(sq_done.values()))])
    pl.vop(lambda e: e.tensor_scalar(tstat[:, :], tstat[:, :], 1.0 / D, None,
                                     ALU.mult))
    pl.vop(lambda e: e.tensor_tensor(rN8[:, :], muN8[:, :], muN8[:, :],
                                     ALU.mult))
    pl.vop(lambda e: e.tensor_tensor(tstat[:, :], tstat[:, :], rN8[:, :],
                                     ALU.subtract))
    nst = pl.v("sDV")
    pl.aop(lambda e: e.activation(tstat[:, :], tstat[:, :], AF.Ln,
                                  bias=c_eps), w=[("sDV", nst)])
    pl.aop(lambda e: e.activation(rN8[:, :], tstat[:, :], AF.Exp,
                                  scale=c_nh))
    rn_done = pl.v("sAC")
    pl.vop(lambda e: e.tensor_tensor(ngN8[:, :], muN8[:, :], rN8[:, :],
                                     ALU.mult), w=[("sAC", rn_done)])
    pl.vop(lambda e: e.tensor_scalar_mul(ngN8[:, :], ngN8[:, :], -1.0))
    ng_done = pl.v("sDV")
    last_tmpn = [("sAC", pl.v("sAC"))]
    for dt in range(DT):
        pl.vop(lambda e, dt=dt: e.scalar_tensor_tensor(
            xbuf[:, dt, :], attn_out[:, dt, :], rN8[:, dt:dt + 1],
            xbuf[:, dt, :], ALU.mult, ALU.add,
            accum_out=s1acc[:, dt:dt + 1]), w=[("sAC", rn_done)])
        x1_done[dt] = pl.v("sDV")
        if dt % 2 == 0:
            w_ = [("sDV", x1_done[dt])]
            if last_tmpn[0][0] == "sDV":
                w_.append(last_tmpn[0])
            pl.aop(lambda e, dt=dt: e.activation(
                tmpn[:, :], xbuf[:, dt, :], AF.Square,
                accum_out=s2acc[:, dt:dt + 1]), w=w_)
            last_tmpn[0] = ("sAC", pl.v("sAC"))
        else:
            w_ = [last_tmpn[0]] if last_tmpn[0][0] == "sAC" else []
            pl.vop(lambda e, dt=dt: e.scalar_tensor_tensor(
                tmpn[:, :], xbuf[:, dt, :], 1.0, xbuf[:, dt, :],
                ALU.mult, ALU.mult, accum_out=s2acc[:, dt:dt + 1]), w=w_)
            last_tmpn[0] = ("sDV", pl.v("sDV"))
        sp_done.append(pl.dma("sSP", "sync",
                              t_y.ap()[dt * P:(dt + 1) * P, :],
                              xbuf[:, dt, :], w=[("sDV", x1_done[dt])]))
    x1sq_ac = pl.v("sAC")

    pl.vop(lambda e: e.tensor_scalar(muF8[:, :], s1acc[:, :], 1.0 / D, None,
                                     ALU.mult))
    pl.vop(lambda e: e.tensor_scalar(lnt[:, :], s2acc[:, :], 1.0 / D, None,
                                     ALU.mult), w=[("sAC", x1sq_ac)])
    pl.vop(lambda e: e.tensor_tensor(rF[:, :], muF8[:, :], muF8[:, :],
                                     ALU.mult))
    pl.vop(lambda e: e.tensor_tensor(lnt[:, :], lnt[:, :], rF[:, :],
                                     ALU.subtract))
    fstat = pl.v("sDV")
    pl.aop(lambda e: e.activation(lnt[:, :], lnt[:, :], AF.Ln, bias=c_eps),
           w=[("sDV", fstat)])
    pl.aop(lambda e: e.activation(rF[:, :], lnt[:, :], AF.Exp, scale=c_nh))
    rf_done = pl.v("sAC")
    pl.vop(lambda e: e.tensor_tensor(ngF8[:, :], muF8[:, :], rF[:, :],
                                     ALU.mult), w=[("sAC", rf_done)])
    pl.vop(lambda e: e.tensor_scalar_mul(ngF8[:, :], ngF8[:, :], -1.0))
    ngf_done = pl.v("sDV")
    SP = pl.v("sSP")

    pl.emit(nc, sems)
    s1.close()
    s01.close()
    ps01.close()
    sT.close()

    sTl = ExitStack()
    lnfT = sTl.enter_context(nc.sbuf_tensor("lnfT", [P, HT, R], BF16,
                                            side="right"))
    sW2 = ExitStack()
    gw_sb = sW2.enter_context(nc.sbuf_tensor("gw_sb", [P, 4, D], BF16,
                                             side="right"))
    uw_sb = sW2.enter_context(nc.sbuf_tensor("uw_sb", [P, 4, D], BF16,
                                             side="right"))
    g_mm_done = {}
    u_mm_done = {}
    gw_ld = {}
    uw_ld = {}

    def load_chunk(ci):
        s0c = (ci % 2) * 2
        gsem = "sGW0" if ci % 2 == 0 else "sGW1"
        usem = "sUW0" if ci % 2 == 0 else "sUW1"
        wg = [("sPE", g_mm_done[2 * (ci - 2) + 1])] if ci >= 2 else []
        pl.dma(gsem, "gpsimd", gw_sb[:, s0c:s0c + 2, :],
               t_gw.ap()[2 * ci:2 * ci + 2].rearrange(
                   "ii p ht c -> p ii (ht c)"), w=wg)
        gw_ld[ci] = (gsem, pl.v(gsem))
        wu = [("sPE", u_mm_done[2 * (ci - 2) + 1])] if ci >= 2 else []
        pl.dma(usem, "gpsimd", uw_sb[:, s0c:s0c + 2, :],
               t_uw.ap()[2 * ci:2 * ci + 2].rearrange(
                   "ii p ht c -> p ii (ht c)"), w=wu)
        uw_ld[ci] = (usem, pl.v(usem))

    sTx = ExitStack()
    lnfbf = [sTx.enter_context(nc.sbuf_tensor(f"lnfbf{i}", [P, D], BF16))
             for i in range(2)]
    psT = ExitStack()
    ps_tr = [psT.enter_context(nc.psum_tensor(f"ps_tr{i}", [P, 4 * P], BF16))
             for i in range(2)]

    pl.new_block()
    load_chunk(0)
    load_chunk(1)
    tr_done = {}
    ev_done = {}
    for dt in range(DT):
        wcast = []
        if dt >= 2:
            wcast.append(("sPE", tr_done[dt - 2]))
        pl.aop(lambda e, dt=dt: e.activation(
            lnfbf[dt % 2][:, :], xbuf[:, dt, :], AF.Identity,
            scale=rF[:, dt:dt + 1], bias=ngF8[:, dt:dt + 1]), w=wcast)
        cast_done = pl.v("sAC")
        for g4 in range(4):
            pb = ps_tr[g4 % 2]
            wt = [("sAC", cast_done), ("sXI", XI)]
            prev = (dt, g4 - 2) if g4 >= 2 else ((dt - 1, g4 + 2)
                                                 if dt > 0 else None)
            if prev in ev_done:
                wt.append(("sDV", ev_done[prev]))
            for k in range(4):
                nt = g4 * 4 + k
                pl.em("tensor",
                      lambda e, dt=dt, nt=nt, k=k, pb=pb:
                      e.transpose(pb[:, k * P:(k + 1) * P],
                                  lnfbf[dt % 2][:, nt * P:(nt + 1) * P],
                                  ident_sb[:, :]),
                      w=(wt if k == 0 else ()),
                      inc=("sPE" if k == 3 else None))
            tg = pl.v("sPE")
            pl.vop(lambda e, dt=dt, g4=g4, pb=pb: e.tensor_copy(
                lnfT[:, g4 * 4:g4 * 4 + 4, dt * P:(dt + 1) * P],
                pb[:, :].rearrange("p (k c) -> p k c", c=P)),
                w=[("sPE", tg)])
            ev_done[(dt, g4)] = pl.v("sDV")
        tr_done[dt] = pl.v("sPE")

    pl.emit(nc, sems)
    sTx.close()
    sx.close()
    psT.close()

    s2 = ExitStack()
    acc = s2.enter_context(nc.sbuf_tensor("acc", [P, DT, D], F32))
    dsB = s2.enter_context(nc.sbuf_tensor("dsB", [P, D], F32))
    s2w = ExitStack()
    dwb = s2w.enter_context(nc.sbuf_tensor("dwb", [P, 8, D], BF16))
    hdn = s2w.enter_context(nc.sbuf_tensor("hdn", [P, 8, R], BF16))
    sg = s2w.enter_context(nc.sbuf_tensor("sg", [P, 2, R], BF16))
    w2b = s2w.enter_context(nc.sbuf_tensor("w2b", [P, 2, D], F32))
    ps2 = ExitStack()
    ps_f = ps2.enter_context(nc.psum_tensor("ps_f", [P, 8, MQ], F32))

    pl.new_block()

    pl.dma("sDS", "scalar", dsB[:, :], t_ds.ap(), w=[("sSP", SP)])
    dsb_done = ("sDS", pl.v("sDS"))

    dw_ld = {}
    silu_done = {}
    hdn_done = {}
    down_mm = {}
    accadd = {}
    dn_prev = {0: None, 4: None}
    out_dma = {}

    def emit_finale(tt):
        pl.vop(lambda e, tt=tt: e.scalar_tensor_tensor(
            acc[:, tt, :], acc[:, tt, :], 1.0, dsB[:, :], ALU.mult, ALU.mult,
            accum_out=wsum[:, tt:tt + 1]),
            w=[("sDV", accadd[(len(BLKS) - 1, tt)])])
        w1v = pl.v("sDV")
        pl.aop(lambda e, tt=tt: e.activation(
            w2b[:, tt % 2, :], acc[:, tt, :], AF.Square,
            accum_out=wsq[:, tt:tt + 1]),
            w=[("sDV", w1v)] + ([out_dma[tt - 2]] if tt - 2 in out_dma
                                else []))
        pl.aop(lambda e, tt=tt: e.mul(t1c[:, :], wsum[:, tt:tt + 1], 1.0 / D))
        pl.aop(lambda e, tt=tt: e.mul(t2c[:, :], wsq[:, tt:tt + 1], 1.0 / D))
        pl.aop(lambda e: e.square(t3c[:, :], t1c[:, :]))
        pl.aop(lambda e: e.activation(t3c[:, :], t3c[:, :], AF.Identity,
                                      scale=cst[:, 2:3], bias=t2c[:, :]))
        pl.aop(lambda e, tt=tt: e.activation(
            t3c[:, :], t3c[:, :], AF.Ln, bias=c_eps))
        pl.aop(lambda e, tt=tt: e.activation(
            rW8[:, tt:tt + 1], t3c[:, :], AF.Exp, scale=c_nh))
        pl.aop(lambda e, tt=tt: e.activation(
            t3c[:, :], rW8[:, tt:tt + 1], AF.Identity, scale=t1c[:, :]))
        pl.aop(lambda e, tt=tt: e.activation(
            ngW8[:, tt:tt + 1], t3c[:, :], AF.Identity, scale=cst[:, 2:3],
            bias=ngN8[:, tt:tt + 1]))
        pl.aop(lambda e, tt=tt: e.activation(
            w2b[:, tt % 2, :], acc[:, tt, :], AF.Identity,
            scale=rW8[:, tt:tt + 1], bias=ngW8[:, tt:tt + 1]))
        w2v = pl.v("sAC")
        osem = "sOUT" if tt % 2 == 0 else "sRL"
        pl.dma(osem, "gpsimd", t_y.ap()[tt * P:(tt + 1) * P, :],
               w2b[:, tt % 2, :], w=[("sAC", w2v), ("sSP", SP)], accum=True)
        out_dma[tt] = (osem, pl.v(osem))

    for bi, blk in enumerate(BLKS):
        ibase = sum(BLKS[:bi])
        ci0 = ibase // 2
        if bi > 0:
            load_chunk(ci0)
            if blk > 2:
                load_chunk(ci0 + 1)
        wdw = ([("sPE", down_mm[(bi - 1, 6)])] if bi > 0
               else [("sSP", SP)])
        for g in range(0, blk, 4):
            pl.dma("sDW", "gpsimd", dwb[:, g:g + 4, :],
                   t_dw.ap()[(ibase + g) * P:(ibase + g + 4) * P, :].rearrange(
                       "(li p) h -> p li h", p=P),
                   w=(wdw if g == 0 else ()))
        dw_ld[bi] = pl.v("sDW")

        for li in range(blk):
            gi = ibase + li
            if gi % 2 == 0 and gi // 2 > ci0 + 1:
                load_chunk(gi // 2)
            slot = (gi // 2 % 2) * 2 + gi % 2
            wg = [gw_ld[gi // 2]]
            if gi >= 1:
                wg.append(("sAC", silu_done[gi - 1]))
            if li == 0 and bi > 0:
                wg.append(("sDV", accadd[(bi - 1, 7)]))
            for half in range(2):
                for ht in range(HT):
                    pl.em("tensor",
                          lambda e, slot=slot, half=half, ht=ht:
                          e.matmul(ps_f[:, half, :],
                                   gw_sb[:, slot, ht * P:(ht + 1) * P],
                                   lnfT[:, ht, half * 512:(half + 1) * 512],
                                   start=(ht == 0), stop=(ht == HT - 1)),
                          w=(wg if (half == 0 and ht == 0) else ()),
                          inc=("sPE" if (half == 1 and ht == HT - 1) else None))
            g_mm_done[gi] = pl.v("sPE")
            wu = [uw_ld[gi // 2]]
            if gi >= 1:
                wu.append(("sDV", hdn_done[gi - 1]))
            if li == 0 and bi > 0:
                wu.append(("sDV", accadd[(bi - 1, 7)]))
            for half in range(2):
                for ht in range(HT):
                    pl.em("tensor",
                          lambda e, slot=slot, half=half, ht=ht:
                          e.matmul(ps_f[:, 2 + half, :],
                                   uw_sb[:, slot, ht * P:(ht + 1) * P],
                                   lnfT[:, ht, half * 512:(half + 1) * 512],
                                   start=(ht == 0), stop=(ht == HT - 1)),
                          w=(wu if (half == 0 and ht == 0) else ()),
                          inc=("sPE" if (half == 1 and ht == HT - 1) else None))
            u_mm_done[gi] = pl.v("sPE")
            ws = [("sPE", g_mm_done[gi]), ("sSM", SM)]
            if gi >= 2:
                ws.append(("sDV", hdn_done[gi - 2]))
            pl.aop(lambda e, gi=gi: e.activation(
                sg[:, gi % 2, :], ps_f[:, 0:2, :], AF.Silu,
                scale=gs_sb[:, gi:gi + 1]), w=ws)
            silu_done[gi] = pl.v("sAC")
            pl.vop(lambda e, gi=gi, li=li: e.scalar_tensor_tensor(
                hdn[:, li, :], ps_f[:, 2:4, :], us_sb[:, gi:gi + 1],
                sg[:, gi % 2, :], ALU.mult, ALU.mult),
                w=[("sPE", u_mm_done[gi]), ("sAC", silu_done[gi])])
            hdn_done[gi] = pl.v("sDV")

        DSEQ = [1, 0, 3, 2, 5, 4, 7, 6]
        for si, tt in enumerate(DSEQ):
            bs = 4 if tt % 2 == 0 else 0
            wdn = [("sDV", hdn_done[ibase + blk - 1]), ("sDW", dw_ld[bi])]
            prev = dn_prev[bs]
            dn_prev[bs] = (bi, tt)
            if prev in accadd:
                wdn.append(("sDV", accadd[prev]))
            if bi == 0 and tt == 0:
                wdn.append(dsb_done)
            if tt == 1:
                wdn.append(("sAC", silu_done[ibase + blk - 1]))
            for li in range(blk):
                for hc in range(4):
                    pl.em("tensor",
                          lambda e, li=li, hc=hc, tt=tt, bs=bs, blk=blk:
                          e.matmul(ps_f[:, bs + hc, :],
                                   hdn[:, li, tt * P:(tt + 1) * P],
                                   dwb[:, li, hc * 512:(hc + 1) * 512],
                                   start=(li == 0), stop=(li == blk - 1)),
                          w=(wdn if (li == 0 and hc == 0) else ()),
                          inc=("sPE" if (li == blk - 1 and hc == 3) else None))
            down_mm[(bi, tt)] = pl.v("sPE")
            if bi == 0:
                pl.vop(lambda e, tt=tt, bs=bs: e.tensor_copy(
                    acc[:, tt, :].rearrange("p (a b) -> p a b", b=512),
                    ps_f[:, bs:bs + 4, :]),
                    w=[("sPE", down_mm[(bi, tt)])])
            else:
                pl.vop(lambda e, tt=tt, bs=bs: e.tensor_tensor(
                    acc[:, tt, :].rearrange("p (a b) -> p a b", b=512),
                    acc[:, tt, :].rearrange("p (a b) -> p a b", b=512),
                    ps_f[:, bs:bs + 4, :], ALU.add),
                    w=[("sPE", down_mm[(bi, tt)])])
            accadd[(bi, tt)] = pl.v("sDV")
            if bi == len(BLKS) - 1 and si >= 1:
                emit_finale(DSEQ[si - 1])

    emit_finale(DSEQ[-1])
    pl.em("sync", fn=None, w=[out_dma[6], out_dma[7]])

    pl.emit(nc, sems)
    s2w.close()
    sW2.close()
    sTl.close()


    s2.close()
    ps2.close()
    glob.close()
    es.close()
    return nc


def kernel(L0_attn_Z_in, L0_attn_A_l, L0_attn_r_l, L0_attn_B_l, L0_attn_C_l,
           L0_ffn_gate_w, L0_ffn_up_w, L0_ffn_down_w, L0_ffn_gate_s,
           L0_ffn_up_s, L0_ffn_down_s):
    f32 = np.float32
    Z = np.asarray(L0_attn_Z_in, dtype=f32)
    A_l = np.asarray(L0_attn_A_l, dtype=f32)
    r_l = np.asarray(L0_attn_r_l, dtype=f32)
    B_l = np.asarray(L0_attn_B_l, dtype=f32)
    C_l = np.asarray(L0_attn_C_l, dtype=f32)
    gate_w = np.asarray(L0_ffn_gate_w, dtype=f32)
    up_w = np.asarray(L0_ffn_up_w, dtype=f32)
    down_w = np.asarray(L0_ffn_down_w, dtype=f32)
    gate_s = np.asarray(L0_ffn_gate_s, dtype=f32)
    up_s = np.asarray(L0_ffn_up_s, dtype=f32)
    down_s = np.asarray(L0_ffn_down_s, dtype=f32)

    def pack_w(w):
        return np.ascontiguousarray(
            w.T.reshape(HT, P, IT, P).transpose(2, 1, 0, 3))

    gw = pack_w(gate_w)
    uw = pack_w(up_w)
    dw = np.ascontiguousarray(down_w.T)
    gs = np.ascontiguousarray(gate_s.reshape(IT, P).T)
    us = np.ascontiguousarray(up_s.reshape(IT, P).T)
    ds = np.ascontiguousarray(np.broadcast_to(down_s.reshape(1, D), (P, D)))
    blT = np.ascontiguousarray(B_l.T)
    clT = np.ascontiguousarray(C_l.T)
    rl = np.ascontiguousarray(np.broadcast_to(r_l.reshape(1, 1), (P, 1)))
    lastm = np.ones((P, 1), f32)
    lastm[P - 1, 0] = 0.0
    ident = np.eye(P, dtype=np.float32).astype(ml_dtypes.bfloat16)

    nc = build_nc()

    in_maps = []
    for c in range(8):
        b, h = c // 2, c % 2
        xr = np.ascontiguousarray(Z[b, h * R:(h + 1) * R, :])
        xT = np.ascontiguousarray(xr.T)
        xtop = np.ascontiguousarray(Z[b, :J, :])
        xtopT = np.ascontiguousarray(xtop.T)
        amixT = np.zeros((J, R), f32)
        maskc = np.ones((P, DT), f32)
        if h == 0:
            amixT[:, :J] = A_l.T
            maskc[:, 0:JT] = 0.0
        in_maps.append({
            "x": xr, "xT": xT, "xtop": xtop, "xtopT": xtopT,
            "blT": blT, "clT": clT, "amixT": amixT,
            "rl": rl, "maskc": maskc, "lastm": lastm, "ident": ident,
            "gw": gw, "uw": uw, "dw": dw,
            "gsc": gs, "usc": us, "dsr": ds,
        })

    res = run_bass_kernel_spmd(nc, in_maps, core_ids=list(range(8)))
    out = np.empty((4, D, D), f32)
    for c in range(8):
        b, h = c // 2, c % 2
        out[b, h * R:(h + 1) * R, :] = res.results[c]["y"]
    return out


# version 22
# speedup vs baseline: 1.0103x; 1.0003x over previous
from contextlib import ExitStack

import numpy as np
import ml_dtypes

import concourse.bass as bass
import concourse.mybir as mybir
from concourse.bass_utils import run_bass_kernel_spmd

F32 = mybir.dt.float32
BF16 = mybir.dt.bfloat16
AF = mybir.ActivationFunctionType
ALU = mybir.AluOpType

P = 128
D = 2048
R = 1024
DT = R // P
NT = D // P
MQ = 512
NQ = D // MQ
J = 256
JT = J // P
II = 5632
IT = II // P
HT = D // P
EPS = 1e-5
BLKS = [8, 8, 8, 8, 8, 4]

ENGS = ("sync", "gpsimd", "scalar", "vector", "tensor")
SEMS = ("sXI", "sXT", "sSM", "sSP", "sRL", "sRL2", "sOUT", "sXTB", "sXTT",
        "sWB0", "sAMX", "sDS", "sGW0", "sGW1", "sUW0", "sUW1", "sDW", "sPE",
        "sDV", "sAC", "sGP")


class Plan:

    def __init__(self):
        self.cnt = {s: 0 for s in SEMS}
        self.ops = None
        self.last_wait = {e: {} for e in ENGS}

    def new_block(self):
        self.ops = {e: [] for e in ENGS}

    def em(self, eng, fn=None, w=(), inc=None, n=1):
        self.ops[eng].append((tuple(w), fn, inc, n))
        if inc is not None:
            self.cnt[inc] += n
            return self.cnt[inc]
        return None

    def gop(self, fn, w=()):
        return self.em("gpsimd", fn, w=list(w) + [("sGP", self.cnt["sGP"])],
                       inc="sGP")

    def vop(self, fn, w=()):
        return self.em("vector", fn, w=list(w) + [("sDV", self.cnt["sDV"])],
                       inc="sDV")

    def aop(self, fn, w=()):
        return self.em("scalar", fn, w=list(w) + [("sAC", self.cnt["sAC"])],
                       inc="sAC")

    def dma(self, sem, q, out, in_, w=(), accum=False):
        if accum:
            return self.em(q, lambda e, o=out, i=in_: e.dma_start(
                o, i, accum_op=ALU.add), w=w, inc=sem, n=16)
        return self.em(q, lambda e, o=out, i=in_: e.dma_start(o, i), w=w,
                       inc=sem, n=16)

    def v(self, s):
        return self.cnt[s]

    def emit(self, nc, sems):
        ops = self.ops
        with nc.Block() as blk:
            for ename in ENGS:
                if not ops[ename]:
                    continue

                def mk(lst=ops[ename], ename=ename):
                    def f(eng):
                        lw = self.last_wait[ename]
                        for w, fn, inc, n in lst:
                            for s, val in w:
                                if val > lw.get(s, 0):
                                    eng.wait_ge(sems[s], val)
                                    lw[s] = val
                            if fn is not None:
                                ins = fn(eng)
                                if inc is not None:
                                    ins.then_inc(sems[inc], n)
                    return f

                getattr(blk, ename)(mk())


def build_nc():
    nc = bass.Bass("TRN2", target_bir_lowering=False, debug=False)

    t_x = nc.dram_tensor("x", [R, D], F32, kind="ExternalInput")
    t_xT = nc.dram_tensor("xT", [D, R], F32, kind="ExternalInput")
    t_xtop = nc.dram_tensor("xtop", [J, D], F32, kind="ExternalInput")
    t_xtopT = nc.dram_tensor("xtopT", [D, J], F32, kind="ExternalInput")
    t_blT = nc.dram_tensor("blT", [J, J], F32, kind="ExternalInput")
    t_clT = nc.dram_tensor("clT", [J, J], F32, kind="ExternalInput")
    t_amixT = nc.dram_tensor("amixT", [J, R], F32, kind="ExternalInput")
    t_rl = nc.dram_tensor("rl", [P, 1], F32, kind="ExternalInput")
    t_mask = nc.dram_tensor("maskc", [P, DT], F32, kind="ExternalInput")
    t_lastm = nc.dram_tensor("lastm", [P, 1], F32, kind="ExternalInput")
    t_ident = nc.dram_tensor("ident", [P, P], BF16, kind="ExternalInput")
    t_gw = nc.dram_tensor("gw", [IT, P, HT, P], F32, kind="ExternalInput")
    t_uw = nc.dram_tensor("uw", [IT, P, HT, P], F32, kind="ExternalInput")
    t_dw = nc.dram_tensor("dw", [II, D], F32, kind="ExternalInput")
    t_gs = nc.dram_tensor("gsc", [P, IT], F32, kind="ExternalInput")
    t_us = nc.dram_tensor("usc", [P, IT], F32, kind="ExternalInput")
    t_ds = nc.dram_tensor("dsr", [P, D], F32, kind="ExternalInput")
    t_y = nc.dram_tensor("y", [R, D], F32, kind="ExternalOutput")

    es = ExitStack()
    sems = {s: es.enter_context(nc.semaphore(s)) for s in SEMS}
    pl = Plan()

    glob = ExitStack()

    def g_sb(name, shape, dtype=F32):
        return glob.enter_context(nc.sbuf_tensor(name, shape, dtype))

    muvar1 = g_sb("muvar1", [P, DT, 2])
    r1 = g_sb("r1", [P, DT])
    A2 = g_sb("A2", [P, DT])
    A3 = g_sb("A3", [P, DT])
    A4 = g_sb("A4", [P, DT])
    srow = g_sb("srow", [P, DT])
    rl_sb = g_sb("rl_sb", [P, 1])
    mask_sb = g_sb("mask_sb", [P, DT])
    lm_sb = g_sb("lm_sb", [P, 1])
    Xmv = g_sb("Xmv", [P, JT, 2])
    Xr = g_sb("Xr", [P, JT])
    nrmuX = g_sb("nrmuX", [P, JT])
    bn6 = g_sb("bn6", [P, 4, 6])
    lnt = g_sb("lnt", [P, DT])
    sc2 = g_sb("sc2", [P, JT])
    muvF = g_sb("muvF", [P, DT, 2])
    rF = g_sb("rF", [P, DT])
    muvN = g_sb("muvN", [P, 2])
    rN = g_sb("rN", [P, 1])
    ones_c = g_sb("ones_c", [P, 1], BF16)
    ones_r = g_sb("ones_r", [1, P], BF16)
    gs_sb = g_sb("gs_sb", [P, IT])
    us_sb = g_sb("us_sb", [P, IT])
    ident_sb = g_sb("ident_sb", [P, P], BF16)
    muvW = g_sb("muvW", [P, 2])
    rW = g_sb("rW", [P, 1])
    nacc = g_sb("nacc", [P, DT, NQ])
    sqacc = g_sb("sqacc", [P, DT, NQ])
    s1acc = g_sb("s1acc", [P, DT])
    s2acc = g_sb("s2acc", [P, DT])
    muN8 = g_sb("muN8", [P, DT])
    rN8 = g_sb("rN8", [P, DT])
    ngN8 = g_sb("ngN8", [P, DT])
    muF8 = g_sb("muF8", [P, DT])
    ngF8 = g_sb("ngF8", [P, DT])
    t1c = g_sb("t1c", [P, 1])
    t3c = g_sb("t3c", [P, 1])
    tstat = g_sb("tstat", [P, DT])
    t2c = g_sb("t2c", [P, 1])
    wsum = g_sb("wsum", [P, DT])
    wsq = g_sb("wsq", [P, DT])
    ngW = g_sb("ngW", [P, 1])
    nA2 = g_sb("nA2", [P, DT])
    muW8 = g_sb("muW8", [P, DT])
    rW8 = g_sb("rW8", [P, DT])
    ngW8 = g_sb("ngW8", [P, DT])
    cst = g_sb("cst", [P, 3])
    c_eps = cst[:, 0:1]
    c_nh = cst[:, 1:2]

    sx = ExitStack()
    xbuf = sx.enter_context(nc.sbuf_tensor("xbuf", [P, DT, D], F32))
    s01 = ExitStack()
    xTb = s01.enter_context(nc.sbuf_tensor("xTb", [P, NT, R], BF16, side="right"))
    ub = s01.enter_context(nc.sbuf_tensor("ub", [P, JT, D], BF16, side="right"))
    wb = s01.enter_context(nc.sbuf_tensor("wb", [P, JT, D], BF16, side="right"))
    xtopTb = s01.enter_context(
        nc.sbuf_tensor("xtopTb", [P, NT, J], BF16, side="right"))
    amixb = s01.enter_context(
        nc.sbuf_tensor("amixb", [P, JT, R], BF16, side="right"))
    sT = ExitStack()
    attn_out = sT.enter_context(nc.sbuf_tensor("attn_out", [P, DT, D], BF16))
    s0 = ExitStack()
    xtop_sb = s0.enter_context(nc.sbuf_tensor("xtop_sb", [P, JT, D], F32))
    Xn = s0.enter_context(nc.sbuf_tensor("Xn", [P, JT, D], BF16))
    bltb = s0.enter_context(nc.sbuf_tensor("bltb", [P, JT, J], BF16))
    cltb = s0.enter_context(nc.sbuf_tensor("cltb", [P, JT, J], BF16))
    blclf = s0.enter_context(nc.sbuf_tensor("blclf", [P, 2 * JT, J], F32))
    ps01 = ExitStack()
    ps_a = ps01.enter_context(nc.psum_tensor("ps_a", [P, 2, MQ], F32))
    ps_t = ps01.enter_context(nc.psum_tensor("ps_t", [P, 3, MQ], F32))
    ps_sh = ps01.enter_context(nc.psum_tensor("ps_sh", [P, 2, MQ], F32))

    def ln_stats(src2048, muvar_out, w_extra=()):
        for c in range(4):
            pl.vop(lambda e, c=c, s=src2048: e.bn_stats(
                bn6[:, c, :], s[:, c * 512:(c + 1) * 512]), w=w_extra)
        pl.vop(lambda e, m=muvar_out: e.bn_aggr(
            m, bn6[:, :, :].rearrange("p a b -> p (a b)")))

    pl.new_block()

    pl.dma("sXI", "sync", ident_sb[:, :], t_ident.ap())

    pl.dma("sWB0", "scalar", blclf[:, 0:JT, :],
           t_blT.ap().rearrange("(jt p) j -> p jt j", p=P))
    pl.dma("sWB0", "scalar", blclf[:, JT:2 * JT, :],
           t_clT.ap().rearrange("(jt p) j -> p jt j", p=P))
    WB0 = pl.v("sWB0")
    for h in range(2):
        pl.dma("sXT", "scalar", xtop_sb[:, 0, h * 1024:(h + 1) * 1024],
               t_xtop.ap()[0:P, h * 1024:(h + 1) * 1024])
        pl.dma("sXT", "sync", xtop_sb[:, 1, h * 1024:(h + 1) * 1024],
               t_xtop.ap()[P:2 * P, h * 1024:(h + 1) * 1024])
    XT = pl.v("sXT")
    for dt in range(DT):
        pl.dma("sXI", "sync", xbuf[:, dt, :],
               t_x.ap()[dt * P:(dt + 1) * P, :])
    XI = pl.v("sXI")
    pl.dma("sSM", "sync", rl_sb[:, :], t_rl.ap())
    pl.dma("sSM", "sync", mask_sb[:, :], t_mask.ap())
    pl.dma("sSM", "sync", lm_sb[:, :], t_lastm.ap())
    pl.dma("sSM", "sync", gs_sb[:, :], t_gs.ap())
    pl.dma("sSM", "sync", us_sb[:, :], t_us.ap())
    SM = pl.v("sSM")

    for k in range(2):
        pl.dma("sXTT", "gpsimd", xtopTb[:, 8 * k:8 * k + 8, :],
               t_xtopT.ap()[8 * k * P:8 * (k + 1) * P, :].rearrange(
                   "(nt p) j -> p nt j", p=P))
    XTT = pl.v("sXTT")
    pl.dma("sAMX", "gpsimd", amixb[:, :, :],
           t_amixT.ap().rearrange("(jt p) d -> p jt d", p=P))
    AMX = pl.v("sAMX")
    for k in range(4):
        pl.dma("sXTB", "gpsimd", xTb[:, 4 * k:4 * k + 4, :],
               t_xT.ap()[4 * k * P:4 * (k + 1) * P, :].rearrange(
                   "(nt p) d -> p nt d", p=P))
    XTB = pl.v("sXTB")

    pl.vop(lambda e: e.memset(cst[:, 0:1], EPS))
    pl.vop(lambda e: e.memset(cst[:, 1:2], -0.5))
    pl.vop(lambda e: e.memset(cst[:, 2:3], -1.0))
    cstv = pl.v("sDV")
    pl.vop(lambda e: e.memset(ones_c[:, :], 1.0))
    pl.vop(lambda e: e.memset(ones_r[:, :], 1.0))

    pl.aop(lambda e: e.activation(sc2[:, 0:1], cst[:, 0:1], AF.Ln,
                                  bias=c_eps), w=[("sDV", cstv)])
    pl.aop(lambda e: e.activation(sc2[:, 0:1], sc2[:, 0:1], AF.Exp,
                                  scale=c_nh))

    for jt in range(JT):
        ln_stats(xtop_sb[:, jt, :], Xmv[:, jt, :], w_extra=[("sXT", XT)])
    xstat = pl.v("sDV")
    for jt in range(JT):
        pl.vop(lambda e, jt=jt: e.tensor_copy(bltb[:, jt, :],
                                              blclf[:, jt, :]),
               w=([("sWB0", WB0)] if jt == 0 else []))
        pl.vop(lambda e, jt=jt: e.tensor_copy(cltb[:, jt, :],
                                              blclf[:, JT + jt, :]))
    pl.aop(lambda e: e.activation(sc2[:, :], Xmv[:, :, 1], AF.Ln, bias=c_eps),
           w=[("sDV", xstat)])
    pl.aop(lambda e: e.activation(Xr[:, :], sc2[:, :], AF.Exp, scale=c_nh))
    xr_done = pl.v("sAC")
    for jt in range(JT):
        pl.vop(lambda e, jt=jt: e.tensor_scalar(
            Xn[:, jt, :], xtop_sb[:, jt, :],
            Xmv[:, jt, 0:1], Xr[:, jt:jt + 1], ALU.subtract, ALU.mult),
            w=[("sAC", xr_done)])
    xn_done = pl.v("sDV")
    pl.vop(lambda e: e.tensor_tensor(nrmuX[:, :], Xmv[:, :, 0], Xr[:, :],
                                     ALU.mult))
    pl.vop(lambda e: e.tensor_scalar_mul(nrmuX[:, :], nrmuX[:, :], -1.0))

    uw_bank_free = {}
    for (wsb, dst) in ((bltb, ub), (cltb, wb)):
        for it2 in range(JT):
            for c4 in range(4):
                bk = (it2 * 4 + c4) % 2
                w = [("sDV", xn_done)]
                if bk in uw_bank_free:
                    w.append(("sAC", uw_bank_free[bk]))
                for jt in range(JT):
                    pl.em("tensor",
                          lambda e, wsb=wsb, it2=it2, c4=c4, jt=jt, bk=bk:
                          e.matmul(ps_a[:, bk, :],
                                   wsb[:, jt, it2 * P:(it2 + 1) * P],
                                   Xn[:, jt, c4 * 512:(c4 + 1) * 512],
                                   start=(jt == 0), stop=(jt == 1)),
                          w=(w if jt == 0 else ()),
                          inc=("sPE" if jt == 1 else None))
                mm = pl.v("sPE")
                pl.aop(lambda e, dst=dst, it2=it2, c4=c4, bk=bk:
                       e.activation(dst[:, it2, c4 * 512:(c4 + 1) * 512],
                                    ps_a[:, bk, :], AF.Copy),
                       w=[("sPE", mm)])
                uw_bank_free[bk] = pl.v("sAC")
    uw_done_pe = pl.v("sPE")
    uw_done_ac = pl.v("sAC")


    pl.emit(nc, sems)
    s0.close()

    s1 = ExitStack()
    att2 = s1.enter_context(nc.sbuf_tensor("att2", [P, 2, NT, MQ], BF16))
    colsumB = s1.enter_context(nc.sbuf_tensor("colsumB", [P, MQ], F32))
    cs_row = s1.enter_context(nc.sbuf_tensor("cs_row", [1, MQ], BF16))
    txb = s1.enter_context(nc.sbuf_tensor("txb", [P, JT, MQ], BF16))
    txf = s1.enter_context(nc.sbuf_tensor("txf", [P, MQ], F32))
    tmpe = s1.enter_context(nc.sbuf_tensor("tmpe", [P, 2, MQ], F32))
    tmpe2 = s1.enter_context(nc.sbuf_tensor("tmpe2", [P, MQ], F32))
    tmpn = s1.enter_context(nc.sbuf_tensor("tmpn", [P, D], BF16))

    pl.new_block()

    for dt in range(DT):
        ln_stats(xbuf[:, dt, :], muvar1[:, dt, :], w_extra=[("sXI", XI)])
    ln1_done = pl.v("sDV")
    coeffs_done = [0]

    def emit_r1_coeffs():
        pl.aop(lambda e: e.activation(lnt[:, :], muvar1[:, :, 1], AF.Ln,
                                      bias=c_eps), w=[("sDV", ln1_done)])
        pl.aop(lambda e: e.activation(r1[:, :], lnt[:, :], AF.Exp,
                                      scale=c_nh))
        r1_done = pl.v("sAC")
        pl.vop(lambda e: e.tensor_tensor(A2[:, :], muvar1[:, :, 0], r1[:, :],
                                         ALU.mult), w=[("sAC", r1_done)])
        pl.vop(lambda e: e.tensor_scalar(srow[:, :], mask_sb[:, :],
                                         rl_sb[:, 0:1], None, ALU.mult),
               w=[("sSM", SM)])
        pl.vop(lambda e: e.tensor_tensor(A3[:, :], srow[:, :], r1[:, :],
                                         ALU.mult))
        pl.vop(lambda e: e.tensor_tensor(A4[:, :], A2[:, :], srow[:, :],
                                         ALU.mult))
        pl.vop(lambda e: e.tensor_scalar_mul(A4[:, :], A4[:, :], -1.0))
        pl.vop(lambda e: e.tensor_scalar_mul(nA2[:, :], A2[:, :], -1.0))
        coeffs_done[0] = pl.v("sDV")

    relu_cnt = {}
    t0_last = {}
    sh_free = {}
    pst_free = {}
    epi_done = {}
    epi_v2 = {0: 0, 1: 0}
    epi_i = [0]
    sq_done = {}
    sp_done = []
    x1_done = {}
    x1sq_done = {}

    def emit_x1_chain(dt):
        qp = NQ - 1
        qsp = slice(qp * MQ, (qp + 1) * MQ)
        pl.aop(lambda e, dt=dt, qp=qp, qsp=qsp: e.activation(
            tmpn[:, 0:MQ], attn_out[:, dt, qsp], AF.Square,
            accum_out=sqacc[:, dt, qp:qp + 1]),
            w=[("sDV", epi_done[(qp, dt)])])
        sq_done[(qp, dt)] = pl.v("sAC")

    for q in range(NQ):
        qb = q % 2
        qs = slice(q * MQ, (q + 1) * MQ)
        for nt in range(NT):
            bk = nt % 2
            w = []
            if (q, nt - 2) in relu_cnt:
                w.append(relu_cnt[(q, nt - 2)])
            elif q > 0:
                w.append(relu_cnt[(q - 1, NT - 2 + nt)])
            else:
                w.append(("sAC", uw_done_ac))
            for dd in range(JT):
                pl.em("tensor",
                      lambda e, nt=nt, dd=dd, bk=bk, qs=qs:
                      e.matmul(ps_a[:, bk, :], ub[:, dd, nt * P:(nt + 1) * P],
                               wb[:, dd, qs], start=(dd == 0), stop=(dd == 1)),
                      w=(w if dd == 0 else ()),
                      inc=("sPE" if dd == 1 else None))
            mm = pl.v("sPE")
            wr = [("sPE", mm)]
            if q >= 2:
                wr.append(("sPE", t0_last[q - 2]))
            if nt % 2 == 0 or q == 0:
                pl.aop(lambda e, nt=nt, qb=qb, bk=bk:
                       e.activation(att2[:, qb, nt, :], ps_a[:, bk, :],
                                    AF.Relu), w=wr)
                relu_cnt[(q, nt)] = ("sAC", pl.v("sAC"))
            else:
                pl.vop(lambda e, nt=nt, qb=qb, bk=bk:
                       e.tensor_scalar_max(att2[:, qb, nt, :],
                                           ps_a[:, bk, :], 0.0), w=wr)
                relu_cnt[(q, nt)] = ("sDV", pl.v("sDV"))
        att_fix = 0
        if q == NQ - 1:
            pl.vop(lambda e, qb=qb: e.tensor_scalar(
                att2[:, qb, NT - 1, MQ - 1:MQ],
                att2[:, qb, NT - 1, MQ - 1:MQ],
                lm_sb[:, 0:1], None, ALU.mult),
                w=[relu_cnt[(q, NT - 1)], relu_cnt[(q, NT - 2)], ("sSM", SM)])
            att_fix = pl.v("sDV")

        w0 = [relu_cnt[(q, NT - 1)], relu_cnt[(q, NT - 2)]]
        if att_fix:
            w0.append(("sDV", att_fix))
        if 0 in sh_free:
            w0.append(sh_free[0])
        for nt in range(NT):
            pl.em("tensor",
                  lambda e, nt=nt, qb=qb:
                  e.matmul(ps_sh[0:1, 0, :], ones_c[:, :], att2[:, qb, nt, :],
                           start=(nt == 0), stop=(nt == NT - 1)),
                  w=(w0 if nt == 0 else ()),
                  inc=("sPE" if nt == NT - 1 else None))
        cs_mm = pl.v("sPE")
        pl.aop(lambda e: e.activation(cs_row[:, :], ps_sh[0:1, 0, :], AF.Copy),
               w=[("sPE", cs_mm)])
        cs_ev = pl.v("sAC")
        if q == 0:
            emit_r1_coeffs()

        t0_mm_val = {}

        def t0_chain(dt):
            bidx = dt % 3
            wt0 = [("sXTB", XTB)]
            if bidx in pst_free:
                wt0.append(("sDV", pst_free[bidx]))
            for nt in range(NT):
                pl.em("tensor",
                      lambda e, dt=dt, nt=nt, qb=qb, bidx=bidx:
                      e.matmul(ps_t[:, bidx, :],
                               xTb[:, nt, dt * P:(dt + 1) * P],
                               att2[:, qb, nt, :],
                               start=(nt == 0), stop=(nt == NT - 1)),
                      w=(wt0 if nt == 0 else ()),
                      inc=("sPE" if nt == NT - 1 else None))
            t0_mm_val[dt] = pl.v("sPE")

        def epilogue(dt):
            bidx = dt % 3
            sl = epi_i[0] % 2
            epi_i[0] += 1
            pl.aop(lambda e, dt=dt, qs=qs, sl=sl: e.activation(
                tmpe[:, sl, :], xbuf[:, dt, qs], AF.Identity,
                scale=r1[:, dt:dt + 1], bias=nA2[:, dt:dt + 1]),
                w=[("sDV", max(epi_v2[sl], coeffs_done[0]))])
            a1 = pl.v("sAC")
            pl.vop(lambda e, dt=dt, sl=sl: e.scalar_tensor_tensor(
                tmpe2[:, :], colsumB[:, :], A4[:, dt:dt + 1], tmpe[:, sl, :],
                ALU.mult, ALU.add), w=[("sAC", a1)])
            epi_v2[sl] = pl.v("sDV")
            if dt < 2:
                pl.vop(lambda e, dt=dt: e.tensor_tensor(
                    tmpe2[:, :], tmpe2[:, :], ps_sh[:, dt, :], ALU.add),
                    w=[("sPE", aadd_mm[dt])])
                sh_free[dt] = ("sDV", pl.v("sDV"))
            pl.vop(lambda e, dt=dt, qs=qs, bidx=bidx, q=q:
                e.scalar_tensor_tensor(
                    attn_out[:, dt, qs], ps_t[:, bidx, :], A3[:, dt:dt + 1],
                    tmpe2[:, :], ALU.mult, ALU.add,
                    accum_out=nacc[:, dt, q:q + 1]),
                w=[("sPE", t0_mm_val[dt])])
            pst_free[bidx] = pl.v("sDV")
            epi_done[(q, dt)] = pl.v("sDV")
            if q >= 1:
                qp = q - 1
                qsp = slice(qp * MQ, (qp + 1) * MQ)
                pl.aop(lambda e, dt=dt, qp=qp, qsp=qsp: e.activation(
                    tmpn[:, 0:MQ], attn_out[:, dt, qsp], AF.Square,
                    accum_out=sqacc[:, dt, qp:qp + 1]),
                    w=[("sDV", epi_done[(qp, dt)])])
                sq_done[(qp, dt)] = pl.v("sAC")
            if q == NQ - 1:
                emit_x1_chain(dt)

        def tx_chain(jt):
            wtx = ([("sAC", cs_ev)] if jt == 0 else [("sDV", csb_done)])
            wtx.append(("sXTT", XTT))
            for nt in range(NT):
                pl.em("tensor",
                      lambda e, jt=jt, nt=nt, qb=qb:
                      e.matmul(ps_sh[:, jt, :],
                               xtopTb[:, nt, jt * P:(jt + 1) * P],
                               att2[:, qb, nt, :],
                               start=(nt == 0), stop=(nt == NT - 1)),
                      w=(wtx if nt == 0 else ()),
                      inc=("sPE" if nt == NT - 1 else None))
            tx_mm = pl.v("sPE")
            pl.vop(lambda e, jt=jt: e.tensor_scalar(
                txf[:, :], ps_sh[:, jt, :], Xr[:, jt:jt + 1], None, ALU.mult),
                w=[("sPE", tx_mm)])
            pl.vop(lambda e, jt=jt: e.scalar_tensor_tensor(
                txb[:, jt, :], colsumB[:, :], nrmuX[:, jt:jt + 1], txf[:, :],
                ALU.mult, ALU.add))

        t0_chain(2)
        wb1 = [("sAC", cs_ev)]
        if 1 in sh_free:
            wb1.append(sh_free[1])
        pl.em("tensor",
              lambda e: e.matmul(ps_sh[:, 1, :], ones_r[:, :], cs_row[:, :],
                                 start=True, stop=True),
              w=wb1, inc="sPE")
        bc_mm = pl.v("sPE")
        t0_chain(3)
        pl.vop(lambda e: e.tensor_copy(colsumB[:, :], ps_sh[:, 1, :]),
               w=[("sPE", bc_mm)])
        csb_done = pl.v("sDV")
        epilogue(2)
        tx_chain(0)
        t0_chain(4)
        epilogue(3)
        tx_chain(1)
        txb_done = pl.v("sDV")
        t0_chain(5)
        epilogue(4)
        aadd_mm = {}
        for dt in range(2):
            for jt in range(JT):
                pl.em("tensor",
                      lambda e, dt=dt, jt=jt:
                      e.matmul(ps_sh[:, dt, :],
                               amixb[:, jt, dt * P:(dt + 1) * P],
                               txb[:, jt, :], start=(jt == 0), stop=(jt == 1)),
                      w=([("sDV", txb_done), ("sAMX", AMX)]
                         if jt == 0 else ()),
                      inc=("sPE" if jt == 1 else None))
            aadd_mm[dt] = pl.v("sPE")
        t0_chain(6)
        epilogue(5)
        t0_chain(7)
        epilogue(6)
        t0_chain(0)
        epilogue(7)
        t0_chain(1)
        t0_last[q] = t0_mm_val[1]
        epilogue(0)
        epilogue(1)

    pl.vop(lambda e: e.reduce_sum(muN8[:, :], nacc[:, :, :],
                                  mybir.AxisListType.X))
    pl.vop(lambda e: e.tensor_scalar(muN8[:, :], muN8[:, :], 1.0 / D, None,
                                     ALU.mult))
    pl.vop(lambda e: e.reduce_sum(tstat[:, :], sqacc[:, :, :],
                                  mybir.AxisListType.X),
           w=[("sAC", max(sq_done.values()))])
    pl.vop(lambda e: e.tensor_scalar(tstat[:, :], tstat[:, :], 1.0 / D, None,
                                     ALU.mult))
    pl.vop(lambda e: e.tensor_tensor(rN8[:, :], muN8[:, :], muN8[:, :],
                                     ALU.mult))
    pl.vop(lambda e: e.tensor_tensor(tstat[:, :], tstat[:, :], rN8[:, :],
                                     ALU.subtract))
    nst = pl.v("sDV")
    pl.aop(lambda e: e.activation(tstat[:, :], tstat[:, :], AF.Ln,
                                  bias=c_eps), w=[("sDV", nst)])
    pl.aop(lambda e: e.activation(rN8[:, :], tstat[:, :], AF.Exp,
                                  scale=c_nh))
    rn_done = pl.v("sAC")
    pl.vop(lambda e: e.tensor_tensor(ngN8[:, :], muN8[:, :], rN8[:, :],
                                     ALU.mult), w=[("sAC", rn_done)])
    pl.vop(lambda e: e.tensor_scalar_mul(ngN8[:, :], ngN8[:, :], -1.0))
    ng_done = pl.v("sDV")
    last_tmpn = [("sAC", pl.v("sAC"))]
    for dt in range(DT):
        pl.vop(lambda e, dt=dt: e.scalar_tensor_tensor(
            xbuf[:, dt, :], attn_out[:, dt, :], rN8[:, dt:dt + 1],
            xbuf[:, dt, :], ALU.mult, ALU.add,
            accum_out=s1acc[:, dt:dt + 1]), w=[("sAC", rn_done)])
        x1_done[dt] = pl.v("sDV")
        if dt % 2 == 0:
            w_ = [("sDV", x1_done[dt])]
            if last_tmpn[0][0] == "sDV":
                w_.append(last_tmpn[0])
            pl.aop(lambda e, dt=dt: e.activation(
                tmpn[:, :], xbuf[:, dt, :], AF.Square,
                accum_out=s2acc[:, dt:dt + 1]), w=w_)
            last_tmpn[0] = ("sAC", pl.v("sAC"))
        else:
            w_ = [last_tmpn[0]] if last_tmpn[0][0] == "sAC" else []
            pl.vop(lambda e, dt=dt: e.scalar_tensor_tensor(
                tmpn[:, :], xbuf[:, dt, :], 1.0, xbuf[:, dt, :],
                ALU.mult, ALU.mult, accum_out=s2acc[:, dt:dt + 1]), w=w_)
            last_tmpn[0] = ("sDV", pl.v("sDV"))
        sp_done.append(pl.dma("sSP", "sync",
                              t_y.ap()[dt * P:(dt + 1) * P, :],
                              xbuf[:, dt, :], w=[("sDV", x1_done[dt])]))
    x1sq_ac = pl.v("sAC")

    pl.vop(lambda e: e.tensor_scalar(muF8[:, :], s1acc[:, :], 1.0 / D, None,
                                     ALU.mult))
    pl.vop(lambda e: e.tensor_scalar(lnt[:, :], s2acc[:, :], 1.0 / D, None,
                                     ALU.mult), w=[("sAC", x1sq_ac)])
    pl.vop(lambda e: e.tensor_tensor(rF[:, :], muF8[:, :], muF8[:, :],
                                     ALU.mult))
    pl.vop(lambda e: e.tensor_tensor(lnt[:, :], lnt[:, :], rF[:, :],
                                     ALU.subtract))
    fstat = pl.v("sDV")
    pl.aop(lambda e: e.activation(lnt[:, :], lnt[:, :], AF.Ln, bias=c_eps),
           w=[("sDV", fstat)])
    pl.aop(lambda e: e.activation(rF[:, :], lnt[:, :], AF.Exp, scale=c_nh))
    rf_done = pl.v("sAC")
    pl.vop(lambda e: e.tensor_tensor(ngF8[:, :], muF8[:, :], rF[:, :],
                                     ALU.mult), w=[("sAC", rf_done)])
    pl.vop(lambda e: e.tensor_scalar_mul(ngF8[:, :], ngF8[:, :], -1.0))
    ngf_done = pl.v("sDV")
    SP = pl.v("sSP")

    pl.emit(nc, sems)
    s1.close()
    s01.close()
    ps01.close()
    sT.close()

    sTl = ExitStack()
    lnfT = sTl.enter_context(nc.sbuf_tensor("lnfT", [P, HT, R], BF16,
                                            side="right"))
    sW2 = ExitStack()
    gw_sb = sW2.enter_context(nc.sbuf_tensor("gw_sb", [P, 4, D], BF16,
                                             side="right"))
    uw_sb = sW2.enter_context(nc.sbuf_tensor("uw_sb", [P, 4, D], BF16,
                                             side="right"))
    g_mm_done = {}
    u_mm_done = {}
    gw_ld = {}
    uw_ld = {}

    def load_chunk(ci):
        s0c = (ci % 2) * 2
        gsem = "sGW0" if ci % 2 == 0 else "sGW1"
        usem = "sUW0" if ci % 2 == 0 else "sUW1"
        wg = [("sPE", g_mm_done[2 * (ci - 2) + 1])] if ci >= 2 else []
        pl.dma(gsem, "gpsimd", gw_sb[:, s0c:s0c + 2, :],
               t_gw.ap()[2 * ci:2 * ci + 2].rearrange(
                   "ii p ht c -> p ii (ht c)"), w=wg)
        gw_ld[ci] = (gsem, pl.v(gsem))
        wu = [("sPE", u_mm_done[2 * (ci - 2) + 1])] if ci >= 2 else []
        pl.dma(usem, "gpsimd", uw_sb[:, s0c:s0c + 2, :],
               t_uw.ap()[2 * ci:2 * ci + 2].rearrange(
                   "ii p ht c -> p ii (ht c)"), w=wu)
        uw_ld[ci] = (usem, pl.v(usem))

    sTx = ExitStack()
    lnfbf = [sTx.enter_context(nc.sbuf_tensor(f"lnfbf{i}", [P, D], BF16))
             for i in range(2)]
    psT = ExitStack()
    ps_tr = [psT.enter_context(nc.psum_tensor(f"ps_tr{i}", [P, 4 * P], BF16))
             for i in range(2)]

    pl.new_block()
    pl.em("gpsimd", fn=None, w=[("sXTB", XTB)])
    load_chunk(0)
    load_chunk(1)
    tr_done = {}
    ev_done = {}
    for dt in range(DT):
        wcast = []
        if dt >= 2:
            wcast.append(("sPE", tr_done[dt - 2]))
        pl.aop(lambda e, dt=dt: e.activation(
            lnfbf[dt % 2][:, :], xbuf[:, dt, :], AF.Identity,
            scale=rF[:, dt:dt + 1], bias=ngF8[:, dt:dt + 1]), w=wcast)
        cast_done = pl.v("sAC")
        for g4 in range(4):
            pb = ps_tr[g4 % 2]
            wt = [("sAC", cast_done), ("sXI", XI)]
            prev = (dt, g4 - 2) if g4 >= 2 else ((dt - 1, g4 + 2)
                                                 if dt > 0 else None)
            if prev in ev_done:
                wt.append(("sDV", ev_done[prev]))
            for k in range(4):
                nt = g4 * 4 + k
                pl.em("tensor",
                      lambda e, dt=dt, nt=nt, k=k, pb=pb:
                      e.transpose(pb[:, k * P:(k + 1) * P],
                                  lnfbf[dt % 2][:, nt * P:(nt + 1) * P],
                                  ident_sb[:, :]),
                      w=(wt if k == 0 else ()),
                      inc=("sPE" if k == 3 else None))
            tg = pl.v("sPE")
            pl.vop(lambda e, dt=dt, g4=g4, pb=pb: e.tensor_copy(
                lnfT[:, g4 * 4:g4 * 4 + 4, dt * P:(dt + 1) * P],
                pb[:, :].rearrange("p (k c) -> p k c", c=P)),
                w=[("sPE", tg)])
            ev_done[(dt, g4)] = pl.v("sDV")
        tr_done[dt] = pl.v("sPE")

    pl.emit(nc, sems)
    sTx.close()
    sx.close()
    psT.close()

    s2 = ExitStack()
    acc = s2.enter_context(nc.sbuf_tensor("acc", [P, DT, D], F32))
    dsB = s2.enter_context(nc.sbuf_tensor("dsB", [P, D], F32))
    s2w = ExitStack()
    dwb = s2w.enter_context(nc.sbuf_tensor("dwb", [P, 8, D], BF16))
    hdn = s2w.enter_context(nc.sbuf_tensor("hdn", [P, 8, R], BF16))
    sg = s2w.enter_context(nc.sbuf_tensor("sg", [P, 2, R], BF16))
    w2b = s2w.enter_context(nc.sbuf_tensor("w2b", [P, 2, D], F32))
    ps2 = ExitStack()
    ps_f = ps2.enter_context(nc.psum_tensor("ps_f", [P, 8, MQ], F32))

    pl.new_block()

    pl.dma("sDS", "scalar", dsB[:, :], t_ds.ap(), w=[("sSP", SP)])
    dsb_done = ("sDS", pl.v("sDS"))

    dw_ld = {}
    silu_done = {}
    hdn_done = {}
    down_mm = {}
    accadd = {}
    dn_prev = {0: None, 4: None}
    out_dma = {}

    def emit_finale(tt):
        pl.vop(lambda e, tt=tt: e.scalar_tensor_tensor(
            acc[:, tt, :], acc[:, tt, :], 1.0, dsB[:, :], ALU.mult, ALU.mult,
            accum_out=wsum[:, tt:tt + 1]),
            w=[("sDV", accadd[(len(BLKS) - 1, tt)])])
        w1v = pl.v("sDV")
        pl.aop(lambda e, tt=tt: e.activation(
            w2b[:, tt % 2, :], acc[:, tt, :], AF.Square,
            accum_out=wsq[:, tt:tt + 1]),
            w=[("sDV", w1v)] + ([out_dma[tt - 2]] if tt - 2 in out_dma
                                else []))
        pl.aop(lambda e, tt=tt: e.mul(t1c[:, :], wsum[:, tt:tt + 1], 1.0 / D))
        pl.aop(lambda e, tt=tt: e.mul(t2c[:, :], wsq[:, tt:tt + 1], 1.0 / D))
        pl.aop(lambda e: e.square(t3c[:, :], t1c[:, :]))
        pl.aop(lambda e: e.activation(t3c[:, :], t3c[:, :], AF.Identity,
                                      scale=cst[:, 2:3], bias=t2c[:, :]))
        pl.aop(lambda e, tt=tt: e.activation(
            t3c[:, :], t3c[:, :], AF.Ln, bias=c_eps))
        pl.aop(lambda e, tt=tt: e.activation(
            rW8[:, tt:tt + 1], t3c[:, :], AF.Exp, scale=c_nh))
        pl.aop(lambda e, tt=tt: e.activation(
            t3c[:, :], rW8[:, tt:tt + 1], AF.Identity, scale=t1c[:, :]))
        pl.aop(lambda e, tt=tt: e.activation(
            ngW8[:, tt:tt + 1], t3c[:, :], AF.Identity, scale=cst[:, 2:3],
            bias=ngN8[:, tt:tt + 1]))
        pl.aop(lambda e, tt=tt: e.activation(
            w2b[:, tt % 2, :], acc[:, tt, :], AF.Identity,
            scale=rW8[:, tt:tt + 1], bias=ngW8[:, tt:tt + 1]))
        w2v = pl.v("sAC")
        osem = "sOUT" if tt % 2 == 0 else "sRL"
        pl.dma(osem, "gpsimd", t_y.ap()[tt * P:(tt + 1) * P, :],
               w2b[:, tt % 2, :], w=[("sAC", w2v), ("sSP", SP)], accum=True)
        out_dma[tt] = (osem, pl.v(osem))

    for bi, blk in enumerate(BLKS):
        ibase = sum(BLKS[:bi])
        ci0 = ibase // 2
        if bi > 0:
            load_chunk(ci0)
            if blk > 2:
                load_chunk(ci0 + 1)
        wdw = ([("sPE", down_mm[(bi - 1, 6)])] if bi > 0
               else [("sSP", SP)])
        for g in range(0, blk, 4):
            pl.dma("sDW", "gpsimd", dwb[:, g:g + 4, :],
                   t_dw.ap()[(ibase + g) * P:(ibase + g + 4) * P, :].rearrange(
                       "(li p) h -> p li h", p=P),
                   w=(wdw if g == 0 else ()))
        dw_ld[bi] = pl.v("sDW")

        for li in range(blk):
            gi = ibase + li
            if gi % 2 == 0 and gi // 2 > ci0 + 1:
                load_chunk(gi // 2)
            slot = (gi // 2 % 2) * 2 + gi % 2
            wg = [gw_ld[gi // 2]]
            if gi >= 1:
                wg.append(("sAC", silu_done[gi - 1]))
            if li == 0 and bi > 0:
                wg.append(("sDV", accadd[(bi - 1, 7)]))
            for half in range(2):
                for ht in range(HT):
                    pl.em("tensor",
                          lambda e, slot=slot, half=half, ht=ht:
                          e.matmul(ps_f[:, half, :],
                                   gw_sb[:, slot, ht * P:(ht + 1) * P],
                                   lnfT[:, ht, half * 512:(half + 1) * 512],
                                   start=(ht == 0), stop=(ht == HT - 1)),
                          w=(wg if (half == 0 and ht == 0) else ()),
                          inc=("sPE" if (half == 1 and ht == HT - 1) else None))
            g_mm_done[gi] = pl.v("sPE")
            wu = [uw_ld[gi // 2]]
            if gi >= 1:
                wu.append(("sDV", hdn_done[gi - 1]))
            if li == 0 and bi > 0:
                wu.append(("sDV", accadd[(bi - 1, 7)]))
            for half in range(2):
                for ht in range(HT):
                    pl.em("tensor",
                          lambda e, slot=slot, half=half, ht=ht:
                          e.matmul(ps_f[:, 2 + half, :],
                                   uw_sb[:, slot, ht * P:(ht + 1) * P],
                                   lnfT[:, ht, half * 512:(half + 1) * 512],
                                   start=(ht == 0), stop=(ht == HT - 1)),
                          w=(wu if (half == 0 and ht == 0) else ()),
                          inc=("sPE" if (half == 1 and ht == HT - 1) else None))
            u_mm_done[gi] = pl.v("sPE")
            ws = [("sPE", g_mm_done[gi]), ("sSM", SM)]
            if gi >= 2:
                ws.append(("sDV", hdn_done[gi - 2]))
            pl.aop(lambda e, gi=gi: e.activation(
                sg[:, gi % 2, :], ps_f[:, 0:2, :], AF.Silu,
                scale=gs_sb[:, gi:gi + 1]), w=ws)
            silu_done[gi] = pl.v("sAC")
            pl.vop(lambda e, gi=gi, li=li: e.scalar_tensor_tensor(
                hdn[:, li, :], ps_f[:, 2:4, :], us_sb[:, gi:gi + 1],
                sg[:, gi % 2, :], ALU.mult, ALU.mult),
                w=[("sPE", u_mm_done[gi]), ("sAC", silu_done[gi])])
            hdn_done[gi] = pl.v("sDV")

        DSEQ = [1, 0, 3, 2, 5, 4, 7, 6]
        for si, tt in enumerate(DSEQ):
            bs = 4 if tt % 2 == 0 else 0
            wdn = [("sDV", hdn_done[ibase + blk - 1]), ("sDW", dw_ld[bi])]
            prev = dn_prev[bs]
            dn_prev[bs] = (bi, tt)
            if prev in accadd:
                wdn.append(("sDV", accadd[prev]))
            if bi == 0 and tt == 0:
                wdn.append(dsb_done)
            if tt == 1:
                wdn.append(("sAC", silu_done[ibase + blk - 1]))
            for li in range(blk):
                for hc in range(4):
                    pl.em("tensor",
                          lambda e, li=li, hc=hc, tt=tt, bs=bs, blk=blk:
                          e.matmul(ps_f[:, bs + hc, :],
                                   hdn[:, li, tt * P:(tt + 1) * P],
                                   dwb[:, li, hc * 512:(hc + 1) * 512],
                                   start=(li == 0), stop=(li == blk - 1)),
                          w=(wdn if (li == 0 and hc == 0) else ()),
                          inc=("sPE" if (li == blk - 1 and hc == 3) else None))
            down_mm[(bi, tt)] = pl.v("sPE")
            if bi == 0:
                pl.vop(lambda e, tt=tt, bs=bs: e.tensor_copy(
                    acc[:, tt, :].rearrange("p (a b) -> p a b", b=512),
                    ps_f[:, bs:bs + 4, :]),
                    w=[("sPE", down_mm[(bi, tt)])])
            else:
                pl.vop(lambda e, tt=tt, bs=bs: e.tensor_tensor(
                    acc[:, tt, :].rearrange("p (a b) -> p a b", b=512),
                    acc[:, tt, :].rearrange("p (a b) -> p a b", b=512),
                    ps_f[:, bs:bs + 4, :], ALU.add),
                    w=[("sPE", down_mm[(bi, tt)])])
            accadd[(bi, tt)] = pl.v("sDV")
            if bi == len(BLKS) - 1 and si >= 1:
                emit_finale(DSEQ[si - 1])

    emit_finale(DSEQ[-1])
    pl.em("sync", fn=None, w=[out_dma[6], out_dma[7]])

    pl.emit(nc, sems)
    s2w.close()
    sW2.close()
    sTl.close()


    s2.close()
    ps2.close()
    glob.close()
    es.close()
    return nc


def kernel(L0_attn_Z_in, L0_attn_A_l, L0_attn_r_l, L0_attn_B_l, L0_attn_C_l,
           L0_ffn_gate_w, L0_ffn_up_w, L0_ffn_down_w, L0_ffn_gate_s,
           L0_ffn_up_s, L0_ffn_down_s):
    f32 = np.float32
    Z = np.asarray(L0_attn_Z_in, dtype=f32)
    A_l = np.asarray(L0_attn_A_l, dtype=f32)
    r_l = np.asarray(L0_attn_r_l, dtype=f32)
    B_l = np.asarray(L0_attn_B_l, dtype=f32)
    C_l = np.asarray(L0_attn_C_l, dtype=f32)
    gate_w = np.asarray(L0_ffn_gate_w, dtype=f32)
    up_w = np.asarray(L0_ffn_up_w, dtype=f32)
    down_w = np.asarray(L0_ffn_down_w, dtype=f32)
    gate_s = np.asarray(L0_ffn_gate_s, dtype=f32)
    up_s = np.asarray(L0_ffn_up_s, dtype=f32)
    down_s = np.asarray(L0_ffn_down_s, dtype=f32)

    def pack_w(w):
        return np.ascontiguousarray(
            w.T.reshape(HT, P, IT, P).transpose(2, 1, 0, 3))

    gw = pack_w(gate_w)
    uw = pack_w(up_w)
    dw = np.ascontiguousarray(down_w.T)
    gs = np.ascontiguousarray(gate_s.reshape(IT, P).T)
    us = np.ascontiguousarray(up_s.reshape(IT, P).T)
    ds = np.ascontiguousarray(np.broadcast_to(down_s.reshape(1, D), (P, D)))
    blT = np.ascontiguousarray(B_l.T)
    clT = np.ascontiguousarray(C_l.T)
    rl = np.ascontiguousarray(np.broadcast_to(r_l.reshape(1, 1), (P, 1)))
    lastm = np.ones((P, 1), f32)
    lastm[P - 1, 0] = 0.0
    ident = np.eye(P, dtype=np.float32).astype(ml_dtypes.bfloat16)

    nc = build_nc()

    in_maps = []
    for c in range(8):
        b, h = c // 2, c % 2
        xr = np.ascontiguousarray(Z[b, h * R:(h + 1) * R, :])
        xT = np.ascontiguousarray(xr.T)
        xtop = np.ascontiguousarray(Z[b, :J, :])
        xtopT = np.ascontiguousarray(xtop.T)
        amixT = np.zeros((J, R), f32)
        maskc = np.ones((P, DT), f32)
        if h == 0:
            amixT[:, :J] = A_l.T
            maskc[:, 0:JT] = 0.0
        in_maps.append({
            "x": xr, "xT": xT, "xtop": xtop, "xtopT": xtopT,
            "blT": blT, "clT": clT, "amixT": amixT,
            "rl": rl, "maskc": maskc, "lastm": lastm, "ident": ident,
            "gw": gw, "uw": uw, "dw": dw,
            "gsc": gs, "usc": us, "dsr": ds,
        })

    res = run_bass_kernel_spmd(nc, in_maps, core_ids=list(range(8)))
    out = np.empty((4, D, D), f32)
    for c in range(8):
        b, h = c // 2, c % 2
        out[b, h * R:(h + 1) * R, :] = res.results[c]["y"]
    return out
